# version 10
# baseline (speedup 1.0000x reference)
import contextlib
import numpy as np

import concourse.bass as bass
import concourse.mybir as mybir
from concourse.bass_utils import run_bass_kernel_spmd

F32 = mybir.dt.float32
BF16 = mybir.dt.bfloat16
ALU = mybir.AluOpType
AX = mybir.AxisListType
ACTF = mybir.ActivationFunctionType

D = 2048
NSEQ = 8192
NB = 2
NH = 16
NE = 8
NEXP = 6
LN_EPS = 1e-5
DEN_EPS = 1e-6
TOK = 2048
SEG = 1024
NSEG = NSEQ // SEG
FT = 512
NFT = TOK // FT
BIG = 1.0e30


class Prog:

    ENGS = ("tensor", "vector", "scalar", "gpsimd", "sync")
    AUTO = {"tensor": "pe", "vector": "ve", "scalar": "se"}

    def __init__(self, nc, block, stack):
        self.nc = nc
        self.block = block
        self.stack = stack
        self.sems = {}
        self.q = {e: [] for e in self.ENGS}

    def sem(self, *names):
        for n in names:
            self.sems[n] = [self.stack.enter_context(self.nc.semaphore(n)), 0]

    def cnt(self, name):
        return self.sems[name][1]

    def _waits(self, wait):
        waits = []
        for w in (wait or ()):
            if isinstance(w, tuple):
                n = w[0]
                v = w[1] if len(w) > 1 else None
            else:
                n, v = w, None
            waits.append((self.sems[n][0], self.cnt(n) if v is None else v))
        return tuple(waits)

    def op(self, eng, make, inc=None, inc_n=1, wait=None):
        waits = self._waits(wait)
        if inc is None and eng in self.AUTO:
            inc = self.AUTO[eng]
            inc_n = 1
        upd = None
        if inc is not None:
            s = self.sems[inc]
            s[1] += inc_n
            upd = (s[0], inc_n)

        def emit(e, make=make, waits=waits, upd=upd):
            for h, v in waits:
                if v > 0:
                    e.wait_ge(h, v)
            inst = make(e)
            if upd is not None:
                inst.then_inc(upd[0], upd[1])

        self.q[eng].append(emit)

    def wait(self, eng, *names_vals):
        waits = self._waits(names_vals)

        def emit(e, waits=waits):
            for h, v in waits:
                if v > 0:
                    e.wait_ge(h, v)

        self.q[eng].append(emit)

    def flush(self):
        for name in self.ENGS:
            fs = self.q[name]
            if not fs:
                continue
            self.q[name] = []
            getattr(self.block, name)(lambda e, fs=tuple(fs): [f(e) for f in fs])

    def barrier(self, sems, engines=ENGS):
        for eng in engines:
            self.wait(eng, *sems)

    def set_warm(self, psum_ap, lhs_ap, rhs_ap):
        self._warm = (psum_ap, lhs_ap, rhs_ap)

    def keepwarm(self, *wait):
        ps, lh, rh = self._warm
        self.op("tensor", lambda t: t.matmul(ps, lh, rh, start=True, stop=True),
                wait=list(wait))


def build_nc(debug=False):
    nc = bass.Bass("TRN2", target_bir_lowering=False, debug=False)
    skind = "ExternalOutput" if debug else "Internal"

    dt = nc.dram_tensor
    xc = dt("xc", [D, NSEQ], F32, kind="ExternalInput").ap()
    xq = dt("xq", [D, TOK], F32, kind="ExternalInput").ap()
    wqTp = dt("wqTp", [128, 16, 16, 128], F32, kind="ExternalInput").ap()
    wkTp = dt("wkTp", [128, 16, 16, 128], F32, kind="ExternalInput").ap()
    wvTp = dt("wvTp", [128, 4, 16, 512], F32, kind="ExternalInput").ap()
    woTp = dt("woTp", [128, 16, 16, 128], F32, kind="ExternalInput").ap()
    ckhi = dt("ckhi", [128, NSEQ], F32, kind="ExternalInput").ap()
    cklo = dt("cklo", [128, NSEQ], F32, kind="ExternalInput").ap()
    skhi = dt("skhi", [128, NSEQ], F32, kind="ExternalInput").ap()
    sklo = dt("sklo", [128, NSEQ], F32, kind="ExternalInput").ap()
    cqhi = dt("cqhi", [128, TOK], F32, kind="ExternalInput").ap()
    cqlo = dt("cqlo", [128, TOK], F32, kind="ExternalInput").ap()
    sqhi = dt("sqhi", [128, TOK], F32, kind="ExternalInput").ap()
    sqlo = dt("sqlo", [128, TOK], F32, kind="ExternalInput").ap()
    ident_in = dt("ident", [128, 128], BF16, kind="ExternalInput").ap()
    ident32_in = dt("ident32", [128, 128], F32, kind="ExternalInput").ap()
    dmask_in = dt("dmask", [128, 16, 512], BF16, kind="ExternalInput").ap()

    wgp = dt("wgp", [2, 128, 16, 8], F32, kind="ExternalInput").ap()
    bgt = dt("bgt", [8, 2], F32, kind="ExternalInput").ap()
    wupp = dt("wupp", [NEXP, 128, 8, 16, 128], F32, kind="ExternalInput").ap()
    bupt = dt("bupt", [128, NEXP * 8], F32, kind="ExternalInput").ap()
    wswp = dt("wswp", [NEXP, 128, 8, 8, 128], F32, kind="ExternalInput").ap()
    bswt = dt("bswt", [128, NEXP * 8], F32, kind="ExternalInput").ap()
    wdnp = dt("wdnp", [16, 128, NEXP, 8, 128], F32, kind="ExternalInput").ap()
    bdn_in = dt("bdn", [NEXP, D], F32, kind="ExternalInput").ap()
    wsup = dt("wsup", [128, 16, 16, 128], F32, kind="ExternalInput").ap()
    bsut = dt("bsut", [128, 16], F32, kind="ExternalInput").ap()
    wsap = dt("wsap", [128, 8, 8, 128], F32, kind="ExternalInput").ap()
    bsat = dt("bsat", [128, 8], F32, kind="ExternalInput").ap()
    wsdp = dt("wsdp", [16, 128, 16, 128], F32, kind="ExternalInput").ap()
    bsdt = dt("bsdt", [128, 16], F32, kind="ExternalInput").ap()

    out = dt("out", [D, TOK], F32, kind="ExternalOutput").ap()

    Ks = dt("Ks", [16, 128, NSEQ], BF16, kind=skind).ap()
    Vs = dt("Vs", [16, 128, 64, 128], BF16, kind=skind).ap()
    Qs = dt("Qs", [16, 128, TOK], BF16, kind=skind).ap()
    y1s = dt("y1s", [D, TOK], F32, kind=skind).ap()
    xt2s = dt("xt2s", [D, FT], F32, kind="Internal").ap()
    cosk = dt("cosk", [128, NSEQ], F32, kind="Internal").ap()
    sink = dt("sink", [128, NSEQ], F32, kind="Internal").ap()
    cosq = dt("cosq", [128, TOK], F32, kind="Internal").ap()
    sinq = dt("sinq", [128, TOK], F32, kind="Internal").ap()

    wqTb = dt("wqTb", [128, 16, 16, 128], BF16, kind="Internal").ap()
    wkTb = dt("wkTb", [128, 16, 16, 128], BF16, kind="Internal").ap()
    wvTb = dt("wvTb", [128, 4, 16, 512], BF16, kind="Internal").ap()
    woTb = dt("woTb", [128, 16, 16, 128], BF16, kind="Internal").ap()
    wgb = dt("wgb", [2, 128, 16, 8], BF16, kind="Internal").ap()
    wupb = dt("wupb", [NEXP, 128, 8, 16, 128], BF16, kind="Internal").ap()
    wswb = dt("wswb", [NEXP, 128, 8, 8, 128], BF16, kind="Internal").ap()
    wdnb = dt("wdnb", [16, 128, NEXP, 8, 128], BF16, kind="Internal").ap()
    bdnb = dt("bdnb", [NEXP, D], BF16, kind="Internal").ap()
    wsub = dt("wsub", [128, 16, 16, 128], BF16, kind="Internal").ap()
    wsab = dt("wsab", [128, 8, 8, 128], BF16, kind="Internal").ap()
    wsdb = dt("wsdb", [16, 128, 16, 128], BF16, kind="Internal").ap()

    xcv = xc.rearrange("(c p) n -> p c n", p=128)
    xqv = xq.rearrange("(c p) n -> p c n", p=128)
    y1v = y1s.rearrange("(c p) n -> p c n", p=128)
    x2v = xt2s.rearrange("(c p) n -> p c n", p=128)
    outv = out.rearrange("(c p) n -> p c n", p=128)

    with contextlib.ExitStack() as stack:
        ctx = stack.enter_context
        block = ctx(nc.Block())
        p = Prog(nc, block, stack)
        p.sem("xdma", "wdma", "odma", "cdma", "c2d0", "c2d1",
              "la0", "la1", "lb0", "lb1", "lh0", "lh1", "st0", "st1",
              "pe", "ve", "se", "gp")

        ones_bf = ctx(nc.sbuf_tensor("ones_bf", [128, 1], BF16))
        ones_f = ctx(nc.sbuf_tensor("ones_f", [128, 1], F32))
        ones1_f = ctx(nc.sbuf_tensor("ones1_f", [1, 128], F32))
        ones1_bf = ctx(nc.sbuf_tensor("ones1_bf", [1, 128], BF16))
        ident = ctx(nc.sbuf_tensor("ident_sb", [128, 128], BF16))
        ident32 = ctx(nc.sbuf_tensor("ident32_sb", [128, 128], F32))
        stat = ctx(nc.sbuf_tensor("stat", [1, 512], F32))
        stat2 = ctx(nc.sbuf_tensor("stat2", [1, 512], F32))
        stat3 = ctx(nc.sbuf_tensor("stat3", [1, 512], F32))
        epsb = ctx(nc.sbuf_tensor("epsb", [1, 1], F32))

        p.op("vector", lambda v: v.memset(ones_bf[:], 1.0))
        p.op("vector", lambda v: v.memset(ones_f[:], 1.0))
        p.op("vector", lambda v: v.memset(ones1_f[:], 1.0))
        p.op("vector", lambda v: v.memset(ones1_bf[:], 1.0))
        p.op("vector", lambda v: v.memset(epsb[:], float(LN_EPS)))
        p.op("sync", lambda s: s.dma_start(ident[:], ident_in), inc="xdma", inc_n=16)
        p.op("sync", lambda s: s.dma_start(ident32[:], ident32_in), inc="xdma",
             inc_n=16)

        def ln_helper(ps_s1, ps_s2, ps_m, ps_r, mb_, rb_, w, soff=0):
            ssl = slice(soff, soff + w)
            p.op("vector", lambda v: v.tensor_scalar(
                stat[0:1, 0:w], ps_s1[0:1, 0:w], 1.0 / D, 0.0,
                op0=ALU.mult, op1=ALU.add), wait=["pe"])
            p.op("vector", lambda v: v.tensor_tensor(
                stat2[0:1, 0:w], stat[0:1, 0:w], stat[0:1, 0:w], op=ALU.mult),
                wait=["ve"])
            p.op("vector", lambda v: v.scalar_tensor_tensor(
                stat3[0:1, 0:w], ps_s2[0:1, 0:w], 1.0 / D, stat2[0:1, 0:w],
                op0=ALU.mult, op1=ALU.subtract), wait=["ve"])
            p.op("scalar", lambda s: s.activation(
                stat2[0:1, 0:w], stat3[0:1, 0:w], ACTF.Sqrt, bias=epsb[0:1, 0:1]),
                wait=["ve"])
            p.op("vector", lambda v: v.reciprocal(stat3[0:1, 0:w], stat2[0:1, 0:w]),
                 wait=["se"])
            p.op("tensor", lambda t: t.matmul(
                ps_m[:, 0:w], ones1_f[:, :], stat[0:1, 0:w], start=True, stop=True),
                wait=["ve"])
            p.op("tensor", lambda t: t.matmul(
                ps_r[:, 0:w], ones1_f[:, :], stat3[0:1, 0:w], start=True, stop=True))
            p.op("vector", lambda v: v.tensor_copy(mb_[:, ssl], ps_m[:, 0:w]),
                 wait=["pe"])
            p.op("vector", lambda v: v.tensor_copy(rb_[:, ssl], ps_r[:, 0:w]))

        with contextlib.ExitStack() as ph0:
            c0 = ph0.enter_context
            rwork = c0(nc.sbuf_tensor("rwork", [128, NSEQ], F32))
            rwork2 = c0(nc.sbuf_tensor("rwork2", [128, NSEQ], F32))
            for hi, lo, dst, n in ((ckhi, cklo, cosk, NSEQ),
                                   (skhi, sklo, sink, NSEQ),
                                   (cqhi, cqlo, cosq, TOK),
                                   (sqhi, sqlo, sinq, TOK)):
                p.op("sync", lambda s, hi=hi, n=n: s.dma_start(rwork[:, 0:n], hi),
                     inc="xdma", inc_n=16, wait=["odma"])
                p.op("sync", lambda s, lo=lo, n=n: s.dma_start(rwork2[:, 0:n], lo),
                     inc="xdma", inc_n=16)
                p.op("vector", lambda v, n=n: v.tensor_tensor(
                    rwork[:, 0:n], rwork[:, 0:n], rwork2[:, 0:n], op=ALU.add),
                    wait=["xdma"])
                p.op("sync", lambda s, dst=dst, n=n: s.dma_start(dst, rwork[:, 0:n]),
                     inc="odma", inc_n=16, wait=["ve"])
            rope_odma = p.cnt("odma")
            p.flush()

        def flatten(ap):
            n = len(ap.shape)
            names = " ".join(chr(ord("a") + i) for i in range(n))
            return ap.rearrange(f"{names} -> ({names})")

        with contextlib.ExitStack() as phb:
            cb = phb.enter_context
            bounce0 = cb(nc.sbuf_tensor("bounce0", [128, 2048], BF16))
            bounce1 = cb(nc.sbuf_tensor("bounce1", [128, 2048], BF16))

            cast_jobs_att = [(wkTp, wkTb), (wqTp, wqTb), (wvTp, wvTb), (woTp, woTb)]
            cast_jobs_ffn = ([(wgp, wgb), (bdn_in, bdnb), (wsup, wsub),
                              (wsap, wsab), (wsdp, wsdb)] +
                             [(wupp[n], wupb[n]) for n in range(NEXP)] +
                             [(wswp[n], wswb[n]) for n in range(NEXP)] +
                             [(wdnp[f], wdnb[f]) for f in range(16)])
            si = [0]
            first_cast = [True]

            def emit_casts(jobs):
                W0 = 2048
                for src, dst in jobs:
                    total = int(np.prod(src.shape))
                    srcf, dstf = flatten(src), flatten(dst)
                    off = 0
                    while off < total:
                        nele = min(128 * W0, total - off)
                        if nele >= W0:
                            nele -= nele % W0
                            shp = (nele // W0, W0)
                        else:
                            shp = (1, nele)
                        i = si[0]
                        sl = (bounce0, bounce1)[i]
                        ssem = ("c2d0", "c2d1")[i]
                        p.op("gpsimd", lambda g, sl=sl, srcf=srcf, off=off, shp=shp:
                             g.dma_start(sl[0:shp[0], 0:shp[1]],
                                         srcf[off:off + shp[0] * shp[1]]
                                         .rearrange("(a b) -> a b", b=shp[1])),
                             inc="cdma", inc_n=16,
                             wait=([(ssem, p.cnt(ssem))] +
                                   ([("odma", rope_odma)] if first_cast[0]
                                    else [])))
                        first_cast[0] = False
                        p.op("gpsimd", lambda g, sl=sl, dstf=dstf, off=off, shp=shp:
                             g.dma_start(dstf[off:off + shp[0] * shp[1]]
                                         .rearrange("(a b) -> a b", b=shp[1]),
                                         sl[0:shp[0], 0:shp[1]]),
                             inc=ssem, inc_n=16, wait=["cdma"])
                        si[0] ^= 1
                        off += shp[0] * shp[1]

            cast_gate = {}
            for job in cast_jobs_att:
                emit_casts([job])
                cast_gate[job[1].name] = (("c2d0", p.cnt("c2d0")),
                                          ("c2d1", p.cnt("c2d1")))
            att_cast_c2 = (p.cnt("c2d0"), p.cnt("c2d1"))
            emit_casts(cast_jobs_ffn)
            ffn_cast_c2 = (p.cnt("c2d0"), p.cnt("c2d1"))
            p.flush()

            with contextlib.ExitStack() as ph1:
                c1 = ph1.enter_context
                xseg = c1(nc.sbuf_tensor("xseg", [128, 16, SEG], F32))
                xqseg = c1(nc.sbuf_tensor("xqseg", [128, 16, 256], F32))
                xln = c1(nc.sbuf_tensor("xln", [128, 16, SEG], BF16))
                xqln = c1(nc.sbuf_tensor("xqln", [128, 16, 256], BF16))
                sqch = c1(nc.sbuf_tensor("sqch", [128, 2, 512], BF16))
                slab_ar = c1(nc.sbuf_tensor("slab_ar", [128, 8192], BF16))
                wsl0 = slab_ar[:, 0:4096].rearrange("p (a b) -> p a b", b=256)
                wsl1 = slab_ar[:, 4096:8192].rearrange("p (a b) -> p a b", b=256)
                vslA = slab_ar[:, 0:8192].rearrange("p (a b) -> p a b", b=512)
                vsl = c1(nc.sbuf_tensor("vsl", [128, 16, 512], BF16))
                ropec = c1(nc.sbuf_tensor("ropec", [128, SEG], F32))
                ropes = c1(nc.sbuf_tensor("ropes", [128, SEG], F32))
                ropecq = c1(nc.sbuf_tensor("ropecq", [128, 256], F32))
                ropesq = c1(nc.sbuf_tensor("ropesq", [128, 256], F32))
                tt1 = c1(nc.sbuf_tensor("tt1", [128, SEG], F32))
                tt2 = c1(nc.sbuf_tensor("tt2", [128, SEG], F32))
                ka = c1(nc.sbuf_tensor("ka", [128, SEG], F32))
                kb = c1(nc.sbuf_tensor("kb", [128, SEG], F32))
                pout = c1(nc.sbuf_tensor("pout", [128, 2, SEG], BF16))
                qall = (xqseg[:].rearrange("p a b -> p (a b)")
                        .bitcast(BF16)[:, 0:4096]
                        .rearrange("p (a b) -> p a b", b=256))
                mb = c1(nc.sbuf_tensor("mb", [128, SEG], F32))
                rb = c1(nc.sbuf_tensor("rb", [128, SEG], F32))
                mbq = tt2[:, 0:256]
                rbq = tt2[:, 256:512]

                ps_s1 = c1(nc.psum_tensor("ps_s1", [1, 512], F32))
                ps_s2 = c1(nc.psum_tensor("ps_s2", [1, 512], F32))
                ps_m = c1(nc.psum_tensor("ps_m", [128, 512], F32))
                ps_r = c1(nc.psum_tensor("ps_r", [128, 512], F32))
                ps_a = c1(nc.psum_tensor("ps_a", [128, SEG], F32))
                ps_b = c1(nc.psum_tensor("ps_b", [128, SEG], F32))
                p.set_warm(ps_s1[0:1, 0:1], ones_bf[0:1, 0:1], ones_bf[0:1, 0:1])

                def ln_block(src, n_t, dst, mb_, rb_, ve_gate=None,
                             xgate=None):
                    for sub in range((n_t + 511) // 512):
                        w = min(512, n_t - sub * 512)
                        sl = slice(sub * 512, sub * 512 + w)
                        sq_prev = [0, 0]
                        for cc in range(16):
                            wv = None
                            if cc == 0:
                                wv = (["ve", "se"] if ve_gate is None
                                      else [("ve", ve_gate), ("se",)])
                                if xgate is not None:
                                    wv.append(xgate[sub])
                            p.op("tensor", lambda t, cc=cc, sl=sl, w=w: t.matmul(
                                ps_s1[0:1, 0:w], ones_f[:, :], src[:, cc, sl],
                                start=(cc == 0), stop=(cc == 15)),
                                wait=wv)
                            p.op("scalar", lambda s, cc=cc, sl=sl, w=w:
                                 s.activation(
                                     sqch[:, cc % 2, 0:w], src[:, cc, sl],
                                     ACTF.Square),
                                 wait=([("pe",)] if cc == 0 else
                                       [("pe", sq_prev[cc % 2])]))
                            p.op("tensor", lambda t, cc=cc, w=w: t.matmul(
                                ps_s2[0:1, 0:w], ones_bf[:, :],
                                sqch[:, cc % 2, 0:w],
                                start=(cc == 0), stop=(cc == 15)), wait=["se"])
                            sq_prev[cc % 2] = p.cnt("pe")
                        ln_helper(ps_s1, ps_s2, ps_m, ps_r, mb_, rb_, w,
                                  soff=sub * 512)
                    helper_ve = p.cnt("ve")
                    chunk_ve = []
                    for cc in range(16):
                        p.op("vector", lambda v, cc=cc: v.tensor_tensor(
                            tt1[:, 0:n_t], src[:, cc, 0:n_t], mb_[:, 0:n_t],
                            op=ALU.subtract), wait=["ve"])
                        p.op("vector", lambda v, cc=cc: v.tensor_tensor(
                            dst[:, cc, 0:n_t], tt1[:, 0:n_t], rb_[:, 0:n_t],
                            op=ALU.mult), wait=["ve"])
                        chunk_ve.append(p.cnt("ve"))
                        if cc % 4 == 3:
                            p.keepwarm(("ve",))
                    return chunk_ve, helper_ve

                prev_pref = None
                for s in range(NSEG):
                    tsl = slice(s * SEG, (s + 1) * SEG)
                    qsl = slice(s * 256, (s + 1) * 256)
                    p.barrier(("pe", "ve", "se", "odma"), engines=("sync",))
                    if prev_pref is None:
                        p.op("sync", lambda sy, tsl=tsl: sy.dma_start(
                            xseg[:, :, 0:512],
                            xcv[:, :, tsl.start:tsl.start + 512]),
                            inc="lh0", inc_n=16)
                        xh0 = ("lh0", p.cnt("lh0"))
                        p.op("sync", lambda sy, tsl=tsl: sy.dma_start(
                            xseg[:, :, 512:1024],
                            xcv[:, :, tsl.start + 512:tsl.stop]),
                            inc="lh1", inc_n=16)
                        xh1 = ("lh1", p.cnt("lh1"))
                        p.op("sync", lambda sy, qsl=qsl: sy.dma_start(
                            xqseg[:], xqv[:, :, qsl]), inc="xdma", inc_n=16)
                    else:
                        xh0, xh1 = prev_pref[0], prev_pref[1]
                    p.op("sync", lambda sy, tsl=tsl: sy.dma_start(
                        ropec[:], cosk[:, tsl]), inc="xdma", inc_n=16,
                        wait=[("odma", rope_odma)])
                    p.op("sync", lambda sy, tsl=tsl: sy.dma_start(
                        ropes[:], sink[:, tsl]), inc="xdma", inc_n=16)
                    p.op("sync", lambda sy, qsl=qsl: sy.dma_start(
                        ropecq[:], cosq[:, qsl]), inc="xdma", inc_n=16)
                    p.op("sync", lambda sy, qsl=qsl: sy.dma_start(
                        ropesq[:], sinq[:, qsl]), inc="xdma", inc_n=16)
                    p.barrier(("xdma",), engines=("vector", "scalar"))
                    xln_ve, h_ve = ln_block(xseg, SEG, xln, mb, rb,
                                            xgate=(xh0, xh1))
                    xqln_ve, _ = ln_block(xqseg, 256, xqln, mbq, rbq,
                                          ve_gate=h_ve)
                    xln_ve = xln_ve

                    def proj_rope(wsrc, src_ln, n_t, rc, rs, dst_hbm, oslice,
                                  first, chunk_ve=None, dst_sb=None):
                        bufs = (wsl0, wsl1)
                        sems = ("la0", "la1")
                        lcnt = [p.cnt("la0"), p.cnt("la1")]
                        for oc in range(8):
                            bi = oc % 2
                            if oc == 0:
                                wv = (list(cast_gate[wsrc.name]) if first
                                      else [])
                                p.op("sync", lambda sy, wsrc=wsrc: sy.dma_start(
                                    bufs[0][:, :, 0:128], wsrc[:, 0, :, :]),
                                    inc=sems[0], inc_n=16,
                                    wait=wv + [("pe",)])
                                p.op("sync", lambda sy, wsrc=wsrc: sy.dma_start(
                                    bufs[0][:, :, 128:256], wsrc[:, 8, :, :]),
                                    inc=sems[0], inc_n=16)
                                lcnt[0] = p.cnt(sems[0])
                            if oc + 1 < 8:
                                nb = (oc + 1) % 2
                                p.op("sync", lambda sy, oc=oc, nb=nb, wsrc=wsrc:
                                     sy.dma_start(bufs[nb][:, :, 0:128],
                                                  wsrc[:, oc + 1, :, :]),
                                     inc=sems[nb], inc_n=16, wait=[("pe",)])
                                p.op("sync", lambda sy, oc=oc, nb=nb, wsrc=wsrc:
                                     sy.dma_start(bufs[nb][:, :, 128:256],
                                                  wsrc[:, oc + 9, :, :]),
                                     inc=sems[nb], inc_n=16)
                                lcnt[nb] = p.cnt(sems[nb])
                            p.wait("tensor", (sems[bi], lcnt[bi]))
                            for half, ps in ((0, ps_a), (1, ps_b)):
                                for sub in range((n_t + 511) // 512):
                                    w = min(512, n_t - sub * 512)
                                    sl = slice(sub * 512, sub * 512 + w)
                                    for kc in range(16):
                                        wv = None
                                        if oc == 0 and chunk_ve is not None:
                                            wv = [("ve", chunk_ve[kc])]
                                            if kc == 0 and half == 0:
                                                wv.append(("se",))
                                        elif kc == 0 and half == 0:
                                            wv = ["se"]
                                        p.op("tensor", lambda t, kc=kc, sl=sl,
                                             half=half, ps=ps, bi=bi: t.matmul(
                                                 ps[:, sl],
                                                 bufs[bi][:, kc,
                                                          128 * half:
                                                          128 * half + 128],
                                                 src_ln[:, kc, sl],
                                                 start=(kc == 0),
                                                 stop=(kc == 15)),
                                             wait=wv)
                            p.op("scalar", lambda s, n_t=n_t: s.copy(
                                ka[:, 0:n_t], ps_a[:, 0:n_t]),
                                wait=["pe", ("ve",)])
                            p.op("scalar", lambda s, n_t=n_t: s.copy(
                                kb[:, 0:n_t], ps_b[:, 0:n_t]))
                            n_ = n_t
                            p.op("vector", lambda v: v.tensor_tensor(
                                tt1[:, 0:n_], ka[:, 0:n_], rc[:, 0:n_],
                                op=ALU.mult), wait=[("odma",), "se", "ve"])
                            p.op("vector", lambda v: v.tensor_tensor(
                                tt2[:, 0:n_], kb[:, 0:n_], rs[:, 0:n_],
                                op=ALU.mult))
                            po0 = (pout[:, 0, 0:n_] if dst_sb is None
                                   else dst_sb[:, oc, :])
                            po1 = (pout[:, 1, 0:n_] if dst_sb is None
                                   else dst_sb[:, oc + 8, :])
                            p.op("vector", lambda v, po0=po0: v.tensor_tensor(
                                po0, tt1[:, 0:n_], tt2[:, 0:n_],
                                op=ALU.subtract), wait=["ve"])
                            p.op("vector", lambda v: v.tensor_tensor(
                                tt1[:, 0:n_], ka[:, 0:n_], rs[:, 0:n_],
                                op=ALU.mult), wait=["ve"])
                            p.op("vector", lambda v: v.tensor_tensor(
                                tt2[:, 0:n_], kb[:, 0:n_], rc[:, 0:n_],
                                op=ALU.mult))
                            p.op("vector", lambda v, po1=po1: v.tensor_tensor(
                                po1, tt1[:, 0:n_], tt2[:, 0:n_],
                                op=ALU.add), wait=["ve"])
                            if dst_sb is None:
                                p.op("sync", lambda sc_, oc=oc: sc_.dma_start(
                                    dst_hbm[oc, :, oslice], pout[:, 0, 0:n_]),
                                    inc="odma", inc_n=16, wait=["ve"])
                                p.op("sync", lambda sc_, oc=oc: sc_.dma_start(
                                    dst_hbm[oc + 8, :, oslice],
                                    pout[:, 1, 0:n_]),
                                    inc="odma", inc_n=16)
                        if dst_sb is not None:
                            p.op("sync", lambda sc_: sc_.dma_start(
                                dst_hbm[:, :, oslice]
                                .rearrange("a p b -> p a b"), dst_sb),
                                inc="odma", inc_n=16, wait=["ve"])

                    if s + 1 < NSEG:
                        nt_ = slice((s + 1) * SEG, (s + 2) * SEG)
                        nq_ = slice((s + 1) * 256, (s + 2) * 256)
                        p.op("sync", lambda sy, nt_=nt_: sy.dma_start(
                            xseg[:, :, 0:512],
                            xcv[:, :, nt_.start:nt_.start + 512]),
                            inc="lh0", inc_n=16, wait=[("ve",)])
                        nxh0 = ("lh0", p.cnt("lh0"))
                        p.op("sync", lambda sy, nt_=nt_: sy.dma_start(
                            xseg[:, :, 512:1024],
                            xcv[:, :, nt_.start + 512:nt_.stop]),
                            inc="lh1", inc_n=16)
                        nxh1 = ("lh1", p.cnt("lh1"))
                        nxt_pref = (nxh0, nxh1, nq_)
                    else:
                        nxt_pref = None
                    proj_rope(wkTb, xln, SEG, ropec, ropes, Ks, tsl, s == 0, xln_ve)
                    proj_rope(wqTb, xqln, 256, ropecq, ropesq, Qs, qsl,
                              s == 0, xqln_ve, dst_sb=qall)
                    if nxt_pref is not None:
                        p.op("sync", lambda sy, nq_=nxt_pref[2]: sy.dma_start(
                            xqseg[:], xqv[:, :, nq_]), inc="xdma", inc_n=16,
                            wait=[("odma",)])
                    prev_pref = nxt_pref

                    vbufs = (vsl, vslA)
                    vsem = ("lb0", "lb1")
                    vcnt = [p.cnt("lb0"), p.cnt("lb1")]

                    def load_v(dvs, bi):
                        gat = (list(cast_gate["wvTb"]) if s == 0 and dvs < 2
                               else [])
                        p.op("sync", lambda sy, dvs=dvs, bi=bi: sy.dma_start(
                            vbufs[bi][:, :, :] if bi == 0 else vbufs[bi],
                            wvTb[:, dvs, :, :]), inc=vsem[bi], inc_n=16,
                            wait=[("pe",)] + gat)
                        vcnt[bi] = p.cnt(vsem[bi])

                    load_v(0, 0)
                    ev_se = [p.cnt("se"), p.cnt("se")]
                    for dvs in range(4):
                        bi = dvs % 2
                        if dvs + 1 < 4:
                            load_v(dvs + 1, bi ^ 1)
                        p.wait("tensor", (vsem[bi], vcnt[bi]))
                        for tb in range(8):
                            half = tb % 2
                            psl = slice(half * 512, half * 512 + 512)
                            for kc in range(16):
                                p.op("tensor", lambda t, tb=tb, kc=kc, psl=psl,
                                     bi=bi: t.matmul(
                                         ps_a[:, psl],
                                         xln[:, kc, tb * 128:tb * 128 + 128],
                                         vbufs[bi][:, kc, :],
                                         start=(kc == 0), stop=(kc == 15)),
                                     wait=([("se", ev_se[half])]
                                           if kc == 0 else None))
                            p.op("scalar", lambda s, psl=psl: s.copy(
                                pout[:, 0, psl], ps_a[:, psl]),
                                wait=["pe", ("odma",)])
                            ev_se[half] = p.cnt("se")
                            p.op("sync", lambda s, tb=tb, dvs=dvs, psl=psl,
                                 kt=s * 8 + tb: s.dma_start(
                                     Vs[4 * dvs:4 * dvs + 4, :, kt, :]
                                     .rearrange("h p d -> p h d"),
                                     pout[:, 0, psl]
                                     .rearrange("p (h d) -> p h d", d=128)),
                                inc="odma", inc_n=16, wait=["se"])
                    p.flush()

            p.barrier(("xdma", "wdma", "odma", "la0", "la1", "lb0", "lb1",
                       "pe", "ve", "se"))
            p.flush()

            with contextlib.ExitStack() as ph23:
                c23 = ph23.enter_context
                prewo = c23(nc.sbuf_tensor("prewo", [128, 16, TOK], BF16))
                dmask = c23(nc.sbuf_tensor("dmask_sb", [128, 16, 512], BF16))
                p.op("sync", lambda s: s.dma_start(dmask[:], dmask_in),
                     inc="xdma", inc_n=16)
                dmask_x = p.cnt("xdma")

                with contextlib.ExitStack() as ph2:
                    c2 = ph2.enter_context
                    khB = [c2(nc.sbuf_tensor(f"kh{i}", [128, NSEQ], BF16))
                           for i in range(2)]
                    qhB = [c2(nc.sbuf_tensor(f"qh{i}", [128, TOK], BF16))
                           for i in range(2)]
                    vtB = [c2(nc.sbuf_tensor(f"vt{i}", [128, 64, 128], BF16))
                           for i in range(2)]
                    s2sb = c2(nc.sbuf_tensor("s2sb", [128, 2048], BF16))
                    s4sb = c2(nc.sbuf_tensor("s4sb", [128, 2048], BF16))
                    rbc = c2(nc.sbuf_tensor("rbc", [128, 512], F32))

                    ps_qkA = c2(nc.psum_tensor("ps_qkA", [128, 1024], F32))
                    ps_qkB = c2(nc.psum_tensor("ps_qkB", [128, 1024], F32))
                    ps_av = c2(nc.psum_tensor("ps_av", [128, 512], F32))
                    ps_den = c2(nc.psum_tensor("ps_den", [1, 512], F32))
                    ps_rbc = c2(nc.psum_tensor("ps_rbc", [128, 512], F32))

                    lh_sem = ("lh0", "lh1")
                    lh_cnt = [0, 0]

                    def load_head(h, bi, wv):
                        hc, off = h // 2, 64 * (h % 2)
                        sm = lh_sem[bi]
                        p.op("sync", lambda sy, hc=hc, off=off, bi=bi: sy.dma_start(
                            khB[bi][0:64, :], Ks[hc, off:off + 64, :]),
                            inc=sm, inc_n=16, wait=wv)
                        p.op("sync", lambda sy, hc=hc, off=off, bi=bi: sy.dma_start(
                            khB[bi][64:128, :], Ks[8 + hc, off:off + 64, :]),
                            inc=sm, inc_n=16)
                        p.op("sync", lambda sy, hc=hc, off=off, bi=bi: sy.dma_start(
                            qhB[bi][0:64, :], Qs[hc, off:off + 64, :]),
                            inc=sm, inc_n=16)
                        p.op("sync", lambda sy, hc=hc, off=off, bi=bi: sy.dma_start(
                            qhB[bi][64:128, :], Qs[8 + hc, off:off + 64, :]),
                            inc=sm, inc_n=16)
                        p.op("sync", lambda sy, h=h, bi=bi: sy.dma_start(
                            vtB[bi][:].rearrange("p k d -> p (k d)"),
                            Vs[h].rearrange("p k d -> p (k d)")),
                            inc=sm, inc_n=16)
                        lh_cnt[bi] = p.cnt(sm)

                    load_head(0, 0, [])
                    for h in range(NH):
                        bi = h % 2
                        kh, qh, vt = khB[bi], qhB[bi], vtB[bi]
                        if h + 1 < NH:
                            load_head(h + 1, (h + 1) % 2, [("pe",)])
                        p.wait("tensor", (lh_sem[bi], lh_cnt[bi]))
                        sq_se = [0, 0]
                        s4_ve = [0, 0]
                        av_pe = [0, 0]
                        for g in range(4):
                            qg = slice(g * 512, g * 512 + 512)
                            nt2 = 8 * (g + 1)

                            def emit_av(b2, nt2=nt2, vt=vt):
                                pi = b2 % 2
                                for j2 in range(2):
                                    kt = 2 * b2 + j2
                                    off = pi * 1024 + j2 * 512
                                    p.op("tensor", lambda t, kt=kt, off=off,
                                         nt2=nt2, vt=vt: t.matmul(
                                             ps_av[:, :], vt[:, kt, :],
                                             s4sb[:, off:off + 512],
                                             start=(kt == 0),
                                             stop=(kt == nt2 * 2 - 1)),
                                         wait=([("ve", s4_ve[pi])]
                                               if j2 == 0 else None))
                                    p.op("tensor", lambda t, kt=kt, off=off,
                                         nt2=nt2: t.matmul(
                                             ps_den[:, :], ones_bf[:, :],
                                             s4sb[:, off:off + 512],
                                             start=(kt == 0),
                                             stop=(kt == nt2 * 2 - 1)))
                                av_pe[pi] = p.cnt("pe")

                            for b2 in range(nt2):
                                pi = b2 % 2
                                psqk = (ps_qkA, ps_qkB)[pi]
                                ssl = slice(pi * 1024, pi * 1024 + 1024)
                                for j2 in range(2):
                                    kt = 2 * b2 + j2
                                    p.op("tensor", lambda t, kt=kt, j2=j2,
                                         psqk=psqk, qg=qg, kh=kh, qh=qh: t.matmul(
                                             psqk[:, j2 * 512:j2 * 512 + 512],
                                             kh[:, kt * 128:kt * 128 + 128],
                                             qh[:, qg], start=True, stop=True),
                                         wait=([("se", sq_se[pi])]
                                               if j2 == 0 else None))
                                qk_pe = p.cnt("pe")
                                if b2 > 1:
                                    emit_av(b2 - 2)
                                p.op("scalar", lambda s, psqk=psqk, ssl=ssl:
                                     s.activation(
                                         s2sb[:, ssl], psqk[:, :], ACTF.Square),
                                     wait=[("pe", qk_pe), ("ve", s4_ve[pi])])
                                sq_se[pi] = p.cnt("se")
                                if b2 >= nt2 - 8:
                                    jb = (b2 - (nt2 - 8)) * 2
                                    p.op("vector", lambda v, jb=jb, ssl=ssl:
                                         v.tensor_tensor(
                                             s4sb[:, ssl], s2sb[:, ssl],
                                             dmask[:, jb:jb + 2, :]
                                             .rearrange("p a b -> p (a b)"),
                                             op=ALU.mult),
                                         wait=["se", ("pe", av_pe[pi]),
                                               ("xdma", dmask_x)])
                                    p.op("vector", lambda v, ssl=ssl:
                                         v.tensor_tensor(
                                             s4sb[:, ssl], s4sb[:, ssl],
                                             s2sb[:, ssl], op=ALU.mult),
                                         wait=["ve"])
                                else:
                                    p.op("vector", lambda v, ssl=ssl:
                                         v.tensor_tensor(
                                             s4sb[:, ssl], s2sb[:, ssl],
                                             s2sb[:, ssl], op=ALU.mult),
                                         wait=["se", ("pe", av_pe[pi])])
                                s4_ve[pi] = p.cnt("ve")
                            if nt2 > 1:
                                emit_av(nt2 - 2)
                            emit_av(nt2 - 1)
                            p.op("vector", lambda v: v.tensor_scalar(
                                stat[:, :], ps_den[:, :], float(DEN_EPS), 0.0,
                                op0=ALU.max, op1=ALU.add), wait=["pe"])
                            p.op("vector", lambda v: v.reciprocal(
                                stat2[:, :], stat[:, :]), wait=["ve"])
                            p.op("tensor", lambda t: t.matmul(
                                ps_rbc[:, :], ones1_f[:, :], stat2[:, :],
                                start=True, stop=True), wait=["ve"])
                            p.op("vector", lambda v: v.tensor_copy(
                                rbc[:, :], ps_rbc[:, :]), wait=["pe"])
                            p.op("vector", lambda v, h=h, qg=qg: v.tensor_tensor(
                                prewo[:, h, qg], ps_av[:, :], rbc[:, :],
                                op=ALU.mult), wait=["ve"])
                            for _i in range(2):
                                av_pe[_i] = max(av_pe[_i], p.cnt("pe"))
                                s4_ve[_i] = max(s4_ve[_i], p.cnt("ve"))
                    p.flush()

                with contextlib.ExitStack() as ph3:
                    c3 = ph3.enter_context
                    wosl = [c3(nc.sbuf_tensor(f"wosl{i}", [128, 16, 128], BF16))
                            for i in range(2)]
                    attn = c3(nc.sbuf_tensor("attn", [128, 16, 512], F32))
                    xqsub = c3(nc.sbuf_tensor("xqsub", [128, 16, 512], F32))
                    sq3 = c3(nc.sbuf_tensor("sq3", [128, 512], BF16))
                    mb3 = c3(nc.sbuf_tensor("mb3", [128, 512], F32))
                    rb3 = c3(nc.sbuf_tensor("rb3", [128, 512], F32))
                    t3B = [c3(nc.sbuf_tensor(f"t3a{i}", [128, 512], F32))
                           for i in range(2)]
                    ps3_o = c3(nc.psum_tensor("ps3_o", [128, 512], F32))
                    ps3_s1 = c3(nc.psum_tensor("ps3_s1", [1, 512], F32))
                    ps3_s2 = c3(nc.psum_tensor("ps3_s2", [1, 512], F32))
                    ps3_m = c3(nc.psum_tensor("ps3_m", [128, 512], F32))
                    ps3_r = c3(nc.psum_tensor("ps3_r", [128, 512], F32))

                    wo_sem = ("la0", "la1")
                    wo_cnt = [p.cnt("la0"), p.cnt("la1")]
                    first3 = [True]

                    def load_wo(oc, buf):
                        p.op("sync", lambda sy, oc=oc, buf=buf: sy.dma_start(
                            wosl[buf][:], woTb[:, oc, :, :]),
                            inc=wo_sem[buf], inc_n=16,
                            wait=([("pe",), ("ve",), ("se",)] +
                                  list(cast_gate["woTb"]) if first3[0]
                                  else [("pe",)]))
                        first3[0] = False
                        wo_cnt[buf] = p.cnt(wo_sem[buf])

                    load_wo(0, 0)
                    for sub in range(4):
                        ssl = slice(sub * 512, sub * 512 + 512)
                        p.barrier(("ve", "se", "odma", "st0", "st1"),
                                  engines=("sync",))
                        p.op("sync", lambda sy, ssl=ssl: sy.dma_start(
                            xqsub[:], xqv[:, :, ssl]), inc="xdma", inc_n=16)
                        for oc in range(16):
                            bi = oc % 2
                            if oc + 1 < 16:
                                load_wo(oc + 1, bi ^ 1)
                            elif sub + 1 < 4:
                                load_wo(0, bi ^ 1)
                            p.wait("tensor", (wo_sem[bi], wo_cnt[bi]))
                            for kc in range(16):
                                p.op("tensor", lambda t, kc=kc, ssl=ssl, bi=bi:
                                     t.matmul(
                                         ps3_o[:, :], wosl[bi][:, kc, :],
                                         prewo[:, kc, ssl],
                                         start=(kc == 0), stop=(kc == 15)),
                                     wait=(["se"] if kc == 0 else None))
                            p.op("scalar", lambda s, oc=oc: s.copy(
                                attn[:, oc, :], ps3_o[:, :]),
                                wait=(["pe", ("ve",)] if oc == 0 else ["pe"]))
                        for cc in range(16):
                            p.op("tensor", lambda t, cc=cc: t.matmul(
                                ps3_s1[:, :], ones_f[:, :], attn[:, cc, :],
                                start=(cc == 0), stop=(cc == 15)),
                                wait=(["se", "ve"] if cc == 0 else None))
                        for cc in range(16):
                            p.op("scalar", lambda s, cc=cc: s.activation(
                                sq3[:, :], attn[:, cc, :], ACTF.Square),
                                wait=["pe"])
                            p.op("tensor", lambda t, cc=cc: t.matmul(
                                ps3_s2[:, :], ones_bf[:, :], sq3[:, :],
                                start=(cc == 0), stop=(cc == 15)), wait=["se"])
                        ln_helper(ps3_s1, ps3_s2, ps3_m, ps3_r, mb3, rb3, 512)
                        p.barrier(("xdma",), engines=("vector",))
                        st_sem = ("st0", "st1")
                        st_od = [p.cnt("st0"), p.cnt("st1")]
                        for cc in range(16):
                            t3a = t3B[cc % 2]
                            p.op("vector", lambda v, cc=cc, t3a=t3a:
                                 v.tensor_tensor(
                                     t3a[:, :], attn[:, cc, :], mb3[:, :],
                                     op=ALU.subtract),
                                 wait=[(st_sem[cc % 2], st_od[cc % 2]), "ve"])
                            p.op("vector", lambda v, cc=cc, t3a=t3a:
                                 v.tensor_tensor(
                                     t3a[:, :], t3a[:, :], rb3[:, :],
                                     op=ALU.mult), wait=["ve"])
                            p.op("vector", lambda v, cc=cc, t3a=t3a:
                                 v.tensor_tensor(
                                     t3a[:, :], t3a[:, :], xqsub[:, cc, :],
                                     op=ALU.add), wait=["ve"])
                            p.op("sync", lambda s, cc=cc, ssl=ssl, t3a=t3a:
                                 s.dma_start(y1v[:, cc, ssl], t3a[:, :]),
                                 inc=st_sem[cc % 2], inc_n=16, wait=["ve"])
                            st_od[cc % 2] = p.cnt(st_sem[cc % 2])
                    p.flush()

        p.barrier(("xdma", "wdma", "odma", "la0", "la1", "lb0", "lb1",
                   "lh0", "lh1", "st0", "st1", "pe", "ve", "se",
                   "c2d0", "c2d1", "cdma"))
        p.flush()

        with contextlib.ExitStack() as ph4:
            c4 = ph4.enter_context
            res = c4(nc.sbuf_tensor("res", [128, 16, FT], BF16))
            xtc = c4(nc.sbuf_tensor("xtc", [128, 16, FT], BF16))
            gexp = c4(nc.sbuf_tensor("gexp", [128, NEXP, 8, FT], BF16))
            hsb = c4(nc.sbuf_tensor("hsb", [128, 8, FT], BF16))
            shs = hsb
            shp_ = c4(nc.sbuf_tensor("shp", [128, 16, FT], BF16))
            ssb = c4(nc.sbuf_tensor("ssb", [128, FT], BF16))
            warena = [c4(nc.sbuf_tensor(f"warena{i}", [128, 6144], BF16))
                      for i in range(2)]
            wupq = [warena[i][:, 0:4096]
                    .rearrange("p (a b c) -> p a b c", b=16, c=128)
                    for i in range(2)]
            wswq = [warena[i][:, 4096:6144]
                    .rearrange("p (a b c) -> p a b c", b=8, c=128)
                    for i in range(2)]
            wdnB = [warena[i][:, 0:6144]
                    .rearrange("p (a b c) -> p a b c", b=8, c=128)
                    for i in range(2)]
            wsuB = [warena[0][:, 0:2048].rearrange("p (a b) -> p a b", b=128),
                    warena[0][:, 2048:4096].rearrange("p (a b) -> p a b", b=128),
                    warena[1][:, 0:2048].rearrange("p (a b) -> p a b", b=128)]
            walias = c4(nc.sbuf_tensor("walias", [128, 8192], BF16))
            wsa_sb = walias[:, 0:8192].rearrange("p (a b c) -> p a b c",
                                                 b=8, c=128)
            wsdB = [c4(nc.sbuf_tensor(f"wsd{i}", [128, 16, 128], BF16))
                    for i in range(2)]
            wg_sb = c4(nc.sbuf_tensor("wg_sb", [128, 2, 16, 8], BF16))
            bg_sb = c4(nc.sbuf_tensor("bg_sb", [8, 2], F32))
            bdn_sb = c4(nc.sbuf_tensor("bdn_sb", [NEXP, D], BF16))
            bup_sb = c4(nc.sbuf_tensor("bup_sb", [128, NEXP * 8], F32))
            bsw_sb = c4(nc.sbuf_tensor("bsw_sb", [128, NEXP * 8], F32))
            bsu_sb = c4(nc.sbuf_tensor("bsu_sb", [128, 16], F32))
            bsa_sb = c4(nc.sbuf_tensor("bsa_sb", [128, 8], F32))
            bsd_sb = c4(nc.sbuf_tensor("bsd_sb", [128, 16], F32))

            gt = c4(nc.sbuf_tensor("gt", [128, 4, 8], F32))
            gw1 = c4(nc.sbuf_tensor("gw1", [128, 4, 8], F32))
            gw2 = c4(nc.sbuf_tensor("gw2", [128, 4, 8], F32))
            gred = c4(nc.sbuf_tensor("gred", [128, 4, 1], F32))
            wsb = c4(nc.sbuf_tensor("wsb", [8, FT], BF16))
            wbc = walias[:, 0:NEXP * FT].rearrange("p (a b) -> p a b", b=FT)
            y1t = c4(nc.sbuf_tensor("y1t", [128, 16, FT], F32))
            osub = c4(nc.sbuf_tensor("osub", [128, FT], F32))
            gate = osub[0:8, :]
            w0 = gexp[:].rearrange("p a b c -> p (a b c)")[0:1, 21504:24576]
            sq4 = ssb
            mb4 = c4(nc.sbuf_tensor("mb4", [128, FT], F32))
            rb4 = c4(nc.sbuf_tensor("rb4", [128, FT], F32))
            t4a = c4(nc.sbuf_tensor("t4a", [128, FT], F32))
            t4b = hsb[:].rearrange("p a b -> p (a b)")[:, 0:1024].bitcast(F32)

            ps4_a = c4(nc.psum_tensor("ps4_a", [128, 512], F32))
            ps4_b = c4(nc.psum_tensor("ps4_b", [128, 512], F32))
            ps4_o = c4(nc.psum_tensor("ps4_o", [128, 512], F32))
            ps4_s1 = c4(nc.psum_tensor("ps4_s1", [1, 512], F32))
            ps4_s2 = c4(nc.psum_tensor("ps4_s2", [1, 512], F32))
            ps4_m = c4(nc.psum_tensor("ps4_m", [128, 512], F32))
            ps4_r = c4(nc.psum_tensor("ps4_r", [128, 512], F32))
            ps4_g = c4(nc.psum_tensor("ps4_g", [128, 512], F32))
            p.set_warm(ps4_s1[0:1, 0:1], ones_bf[0:1, 0:1], ones_bf[0:1, 0:1])

            p.op("sync", lambda sy: sy.dma_start(
                wg_sb[:].rearrange("p a b c -> p a (b c)"),
                wgb.rearrange("a p b c -> p a (b c)")), inc="wdma", inc_n=16)
            p.op("sync", lambda sy: sy.dma_start(bg_sb[:], bgt),
                 inc="wdma", inc_n=16)
            p.op("sync", lambda sy: sy.dma_start(bdn_sb[:], bdnb),
                 inc="wdma", inc_n=16)
            p.op("sync", lambda sy: sy.dma_start(bup_sb[:], bupt),
                 inc="wdma", inc_n=16)
            p.op("sync", lambda sy: sy.dma_start(bsw_sb[:], bswt),
                 inc="wdma", inc_n=16)
            p.op("sync", lambda sy: sy.dma_start(bsu_sb[:], bsut),
                 inc="wdma", inc_n=16)
            p.op("sync", lambda sy: sy.dma_start(bsa_sb[:], bsat),
                 inc="wdma", inc_n=16)
            p.op("sync", lambda sy: sy.dma_start(bsd_sb[:], bsdt),
                 inc="wdma", inc_n=16)
            p.barrier(("wdma",), engines=("tensor", "scalar", "vector"))

            up_sem = ("la0", "la1")
            sw_sem = ("lb0", "lb1")

            y1_x = [0]
            p.op("sync", lambda sy: sy.dma_start(
                y1t[:], y1v[:, :, 0:FT]), inc="xdma", inc_n=16)
            y1_x[0] = p.cnt("xdma")
            for ft in range(NFT):
                fsl = slice(ft * FT, ft * FT + FT)
                p.barrier(("pe", "ve", "se", "odma", "st0", "st1"),
                          engines=("sync",))
                p.wait("tensor", ("xdma", y1_x[0]))
                p.wait("vector", ("xdma", y1_x[0]))
                p.wait("scalar", ("xdma", y1_x[0]))
                for cc in range(16):
                    p.op("tensor", lambda t, cc=cc: t.matmul(
                        ps4_s1[:, :], ones_f[:, :], y1t[:, cc, :],
                        start=(cc == 0), stop=(cc == 15)))
                for cc in range(16):
                    p.op("scalar", lambda s, cc=cc: s.activation(
                        sq4[:, :], y1t[:, cc, :], ACTF.Square), wait=["pe"])
                    p.op("tensor", lambda t, cc=cc: t.matmul(
                        ps4_s2[:, :], ones_bf[:, :], sq4[:, :],
                        start=(cc == 0), stop=(cc == 15)), wait=["se"])
                ln_helper(ps4_s1, ps4_s2, ps4_m, ps4_r, mb4, rb4, FT)
                for cc in range(16):
                    p.op("vector", lambda v, cc=cc: v.tensor_tensor(
                        t4a[:, :], y1t[:, cc, :], mb4[:, :], op=ALU.subtract),
                        wait=(["ve", ("st0",), ("st1",)] if cc == 0
                              else ["ve"]))
                    p.op("vector", lambda v, cc=cc: v.tensor_tensor(
                        res[:, cc, :], t4a[:, :], rb4[:, :], op=ALU.mult),
                        wait=["ve"])
                    if cc % 4 == 3:
                        p.keepwarm(("ve",))

                for j in range(2):
                    xcur = res if j == 0 else xtc
                    for kc in range(16):
                        p.op("tensor", lambda t, kc=kc, j=j, xcur=xcur: t.matmul(
                            ps4_g[0:8, :], wg_sb[:, j, kc, :], xcur[:, kc, :],
                            start=(kc == 0), stop=(kc == 15)),
                            wait=(["ve", "se"] if kc == 0 else None))
                    p.op("scalar", lambda s, j=j: s.activation(
                        gate[:, :], ps4_g[0:8, :], ACTF.Identity,
                        bias=bg_sb[:, j:j + 1]), wait=["pe"])
                    for tt in range(4):
                        p.op("tensor", lambda t, tt=tt: t.transpose(
                            ps4_m[:, tt * 8:tt * 8 + 8],
                            gate[:, tt * 128:tt * 128 + 128],
                            ident32[0:8, 0:8]),
                            wait=(["se", "ve"] if tt == 0 else None))
                    p.op("vector", lambda v: v.tensor_copy(
                        gt[:].rearrange("p a b -> p (a b)"), ps4_m[:, 0:32]),
                        wait=["pe"])
                    p.op("vector", lambda v: v.tensor_reduce(
                        gred[:, :, :], gt[:, :, :], axis=AX.X, op=ALU.min),
                        wait=["ve"])
                    p.op("vector", lambda v: v.tensor_tensor(
                        gw1[:, :, :], gt[:, :, :],
                        gred[:, :, :].broadcast_to([128, 4, 8]),
                        op=ALU.is_equal), wait=["ve"])
                    p.op("vector", lambda v: v.scalar_tensor_tensor(
                        gw2[:, :, :], gw1[:, :, :], BIG, gt[:, :, :],
                        op0=ALU.mult, op1=ALU.add), wait=["ve"])
                    p.keepwarm(("ve",))
                    p.op("vector", lambda v: v.tensor_reduce(
                        gred[:, :, :], gw2[:, :, :], axis=AX.X, op=ALU.min),
                        wait=["ve"])
                    p.op("vector", lambda v: v.tensor_tensor(
                        gw2[:, :, :], gw2[:, :, :],
                        gred[:, :, :].broadcast_to([128, 4, 8]),
                        op=ALU.is_equal), wait=["ve"])
                    p.op("vector", lambda v: v.tensor_tensor(
                        gw1[:, :, :], gw1[:, :, :], gw2[:, :, :], op=ALU.add),
                        wait=["ve"])
                    p.op("vector", lambda v: v.tensor_scalar(
                        gw1[:, :, :], gw1[:, :, :], -1.0, 1.0,
                        op0=ALU.mult, op1=ALU.add), wait=["ve"])
                    p.op("vector", lambda v: v.tensor_reduce(
                        gred[:, :, :], gt[:, :, :], axis=AX.X, op=ALU.max),
                        wait=["ve"])
                    p.op("vector", lambda v: v.tensor_tensor(
                        gw2[:, :, :], gt[:, :, :],
                        gred[:, :, :].broadcast_to([128, 4, 8]),
                        op=ALU.subtract), wait=["ve"])
                    p.op("scalar", lambda s: s.activation(
                        gw2[:].rearrange("p a b -> p (a b)"),
                        gw2[:].rearrange("p a b -> p (a b)"), ACTF.Exp),
                        wait=["ve"])
                    p.keepwarm(("se",))
                    p.op("vector", lambda v: v.tensor_tensor(
                        gw2[:, :, :], gw2[:, :, :], gw1[:, :, :], op=ALU.mult),
                        wait=["se"])
                    p.op("vector", lambda v: v.tensor_reduce(
                        gred[:, :, :], gw2[:, :, :], axis=AX.X, op=ALU.add),
                        wait=["ve"])
                    p.op("vector", lambda v: v.reciprocal(
                        gred[:].rearrange("p a b -> p (a b)"),
                        gred[:].rearrange("p a b -> p (a b)")), wait=["ve"])
                    p.op("vector", lambda v: v.tensor_tensor(
                        gw2[:, :, :], gw2[:, :, :],
                        gred[:, :, :].broadcast_to([128, 4, 8]), op=ALU.mult),
                        wait=["ve"])
                    for tt in range(4):
                        p.op("tensor", lambda t, tt=tt: t.transpose(
                            ps4_g[0:8, tt * 128:tt * 128 + 128],
                            gw2[:].rearrange("p a b -> p (a b)")
                            [:, tt * 8:tt * 8 + 8], ident32[:, :]),
                            wait=(["ve", "se"] if tt == 0 else None))
                    p.op("scalar", lambda s: s.copy(wsb[:, :], ps4_g[0:8, :]),
                         wait=["pe"])
                    for n in range(NEXP):
                        p.op("sync", lambda sy, n=n: sy.dma_start(
                            w0[0:1, n * FT:(n + 1) * FT], wsb[n:n + 1, :]),
                            inc="lh0", inc_n=16,
                            wait=(["se", ("pe",)] if n == 0 else None))
                    p.wait("tensor", ("lh0", p.cnt("lh0")))
                    for n in range(NEXP):
                        p.op("tensor", lambda t, n=n: t.matmul(
                            ps4_m[:, :], ones1_bf[:, :],
                            w0[0:1, n * FT:(n + 1) * FT],
                            start=True, stop=True),
                            wait=(["ve"] if n > 0 else None))
                        p.op("vector", lambda v, n=n: v.tensor_copy(
                            wbc[:, n, :], ps4_m[:, :]), wait=["pe"])

                    up_cnt = [p.cnt("la0"), p.cnt("la1")]
                    sw_cnt = [p.cnt("lb0"), p.cnt("lb1")]

                    def load_upq(n, q, buf):
                        p.op("sync", lambda sy, n=n, q=q, buf=buf: sy.dma_start(
                            wupq[buf].rearrange("p a b c -> p a (b c)"),
                            wupb[n][:, q * 2:q * 2 + 2, :, :]
                            .rearrange("p a b c -> p a (b c)")),
                            inc=up_sem[buf], inc_n=16, wait=[("pe",)])
                        up_cnt[buf] = p.cnt(up_sem[buf])

                    def load_swq(n, q, buf):
                        p.op("sync", lambda sy, n=n, q=q, buf=buf: sy.dma_start(
                            wswq[buf].rearrange("p a b c -> p a (b c)"),
                            wswb[n][:, q * 2:q * 2 + 2, :, :]
                            .rearrange("p a b c -> p a (b c)")),
                            inc=sw_sem[buf], inc_n=16, wait=[("pe",)])
                        sw_cnt[buf] = p.cnt(sw_sem[buf])

                    load_upq(0 if j == 0 else 0, 0, 0) if False else None
                    load_upq(0, 0, 0)
                    load_swq(0, 0, 0)
                    bank_se = [p.cnt("se"), p.cnt("se")]
                    for n in range(NEXP):
                        for ec in range(8):
                            q, buf = ec // 2, (ec // 2) % 2
                            if ec % 2 == 0:
                                if ec < 6:
                                    load_upq(n, q + 1, buf ^ 1)
                                elif n + 1 < NEXP:
                                    load_upq(n + 1, 0, buf ^ 1)
                                p.wait("tensor", (up_sem[buf], up_cnt[buf]))
                            psx = ps4_a if ec % 2 == 0 else ps4_b
                            for kc in range(16):
                                p.op("tensor", lambda t, ec=ec, kc=kc, psx=psx,
                                     buf=buf, xcur=xcur: t.matmul(
                                         psx[:, :], wupq[buf][:, ec % 2, kc, :],
                                         xcur[:, kc, :],
                                         start=(kc == 0), stop=(kc == 15)),
                                     wait=([("se", bank_se[ec % 2])]
                                           if kc == 0 else None))
                            p.op("scalar", lambda s, ec=ec, n=n, psx=psx:
                                 s.activation(
                                     hsb[:, ec, :], psx[:, :], ACTF.Identity,
                                     bias=bup_sb[:, n * 8 + ec:n * 8 + ec + 1]),
                                 wait=["pe", ("ve",)])
                            bank_se[ec % 2] = p.cnt("se")
                        bank_se = [p.cnt("se"), p.cnt("se")]
                        for f2 in range(8):
                            q, buf = f2 // 2, (f2 // 2) % 2
                            if f2 % 2 == 0:
                                if f2 < 6:
                                    load_swq(n, q + 1, buf ^ 1)
                                elif n + 1 < NEXP:
                                    load_swq(n + 1, 0, buf ^ 1)
                                p.wait("tensor", (sw_sem[buf], sw_cnt[buf]))
                            psx = ps4_a if f2 % 2 == 0 else ps4_b
                            for ec in range(8):
                                p.op("tensor", lambda t, f2=f2, ec=ec, psx=psx,
                                     buf=buf: t.matmul(
                                         psx[:, :], wswq[buf][:, f2 % 2, ec, :],
                                         hsb[:, ec, :],
                                         start=(ec == 0), stop=(ec == 7)),
                                     wait=([("se", bank_se[f2 % 2])]
                                           if ec == 0 else None))
                            p.op("scalar", lambda s, f2=f2, n=n, psx=psx:
                                 s.activation(
                                     ssb[:, :], psx[:, :], ACTF.Silu,
                                     bias=bsw_sb[:, n * 8 + f2:n * 8 + f2 + 1]),
                                 wait=["pe", ("ve",)])
                            bank_se[f2 % 2] = p.cnt("se")
                            p.op("vector", lambda v, f2=f2: v.tensor_tensor(
                                ssb[:, :], ssb[:, :], hsb[:, f2, :],
                                op=ALU.mult), wait=["se"])
                            p.op("vector", lambda v, f2=f2, n=n: v.tensor_tensor(
                                gexp[:, n, f2, :], ssb[:, :], wbc[:, n, :],
                                op=ALU.mult), wait=["ve"])

                    su_cnt = [p.cnt("la0"), p.cnt("la1"), p.cnt("lh1")]
                    su_sem = ("la0", "la1", "lh1")
                    su_used = [p.cnt("pe")] * 3

                    def load_su(sc, buf):
                        p.op("sync", lambda sy, sc=sc, buf=buf: sy.dma_start(
                            wsuB[buf], wsub[:, sc, :, :]),
                            inc=su_sem[buf], inc_n=16,
                            wait=[("pe", su_used[buf])])
                        su_cnt[buf] = p.cnt(su_sem[buf])

                    load_su(0, 0)
                    load_su(1, 1)
                    p.op("sync", lambda sy: sy.dma_start(
                        wsa_sb.rearrange("p a b c -> p (a b c)"),
                        wsab.rearrange("p a b c -> p (a b c)")),
                        inc="lb0", inc_n=16, wait=[("ve",)])
                    wsa_ld = p.cnt("lb0")
                    bank_se = [p.cnt("se"), p.cnt("se")]
                    for slot in range(2):
                        for sc8 in range(8):
                            sc = slot * 8 + sc8
                            buf = sc % 3
                            psx = ps4_a if sc8 % 2 == 0 else ps4_b
                            if sc + 2 < 16:
                                load_su(sc + 2, (sc + 2) % 3)
                            p.wait("tensor", (su_sem[buf], su_cnt[buf]))
                            for kc in range(16):
                                p.op("tensor", lambda t, sc=sc, kc=kc, buf=buf,
                                     psx=psx, xcur=xcur: t.matmul(
                                         psx[:, :], wsuB[buf][:, kc, :],
                                         xcur[:, kc, :],
                                         start=(kc == 0), stop=(kc == 15)),
                                     wait=([("se", bank_se[sc8 % 2]), ("ve",)]
                                           if kc == 0 else None))
                            su_used[buf] = p.cnt("pe")
                            p.op("scalar", lambda s, sc=sc, sc8=sc8, psx=psx:
                                 s.activation(
                                     shs[:, sc8, :], psx[:, :], ACTF.Identity,
                                     bias=bsu_sb[:, sc:sc + 1]),
                                 wait=["pe", ("ve",)])
                            bank_se[sc8 % 2] = p.cnt("se")
                        bank_se = [p.cnt("se"), p.cnt("se")]
                        for ec in range(8):
                            psx = ps4_a if ec % 2 == 0 else ps4_b
                            for e2 in range(8):
                                wv = None
                                if e2 == 0:
                                    wv = [("se", bank_se[ec % 2]), ("ve",)]
                                    if ec == 0 and slot == 0:
                                        wv.append(("lb0", wsa_ld))
                                p.op("tensor", lambda t, ec=ec, e2=e2,
                                     psx=psx: t.matmul(
                                         psx[:, :], wsa_sb[:, ec, e2, :],
                                         shs[:, e2, :],
                                         start=(e2 == 0), stop=(e2 == 7)),
                                     wait=wv)
                            p.op("scalar", lambda s, ec=ec, psx=psx: s.activation(
                                ssb[:, :], psx[:, :], ACTF.Silu,
                                bias=bsa_sb[:, ec:ec + 1]), wait=["pe", ("ve",)])
                            bank_se[ec % 2] = p.cnt("se")
                            p.op("vector", lambda v, slot=slot, ec=ec:
                                 v.tensor_tensor(
                                     shp_[:, slot * 8 + ec, :], ssb[:, :],
                                     shs[:, ec, :], op=ALU.mult),
                                 wait=["se"])

                    dn_cnt = [p.cnt("la0"), p.cnt("la1")]

                    def load_dn(fc, buf):
                        p.op("sync", lambda sy, fc=fc, buf=buf: sy.dma_start(
                            wdnB[buf].rearrange("p a b c -> p (a b c)"),
                            wdnb[fc].rearrange("p a b c -> p (a b c)")),
                            inc=up_sem[buf], inc_n=16, wait=[("pe",)])
                        p.op("sync", lambda sy, fc=fc, buf=buf: sy.dma_start(
                            wsdB[buf][:].rearrange("p a b -> p (a b)"),
                            wsdb[fc].rearrange("p a b -> p (a b)")),
                            inc=up_sem[buf], inc_n=16)
                        dn_cnt[buf] = p.cnt(up_sem[buf])

                    load_dn(0, 0)
                    bank_se = [p.cnt("se"), p.cnt("se")]
                    ve_ready = p.cnt("ve")
                    x2_od = [p.cnt("st0"), p.cnt("st1")]
                    for fc in range(16):
                        buf = fc % 2
                        pso = ps4_o if fc % 2 == 0 else ps4_g
                        if fc + 1 < 16:
                            load_dn(fc + 1, buf ^ 1)
                        p.wait("tensor", (up_sem[buf], dn_cnt[buf]))
                        p.op("tensor", lambda t, fc=fc, pso=pso: t.matmul(
                            pso[:, :], bdn_sb[:, fc * 128:fc * 128 + 128],
                            wsb[0:NEXP, :], start=True, stop=False),
                            wait=[("ve", ve_ready), ("se", bank_se[fc % 2])])
                        for n in range(NEXP):
                            for ec in range(8):
                                p.op("tensor", lambda t, n=n, ec=ec, buf=buf,
                                     pso=pso: t.matmul(
                                         pso[:, :], wdnB[buf][:, n, ec, :],
                                         gexp[:, n, ec, :],
                                         start=False, stop=False))
                        for sc in range(16):
                            p.op("tensor", lambda t, sc=sc, buf=buf, pso=pso:
                                 t.matmul(
                                     pso[:, :], wsdB[buf][:, sc, :],
                                     shp_[:, sc, :],
                                     start=False, stop=(sc == 15)))
                        p.op("scalar", lambda s, fc=fc, pso=pso: s.activation(
                            osub[:, :], pso[:, :], ACTF.Identity,
                            bias=bsd_sb[:, fc:fc + 1]),
                            wait=["pe", ("ve",), ("xdma",)])
                        bank_se[fc % 2] = p.cnt("se")
                        if j == 0:
                            p.op("vector", lambda v, fc=fc: v.tensor_tensor(
                                xtc[:, fc, :], osub[:, :], res[:, fc, :],
                                op=ALU.add), wait=["se", ("pe",)])
                        else:
                            tx = t4a if fc % 2 == 0 else t4b
                            p.op("vector", lambda v, fc=fc, tx=tx:
                                 v.tensor_tensor(
                                     tx[:, :], osub[:, :], res[:, fc, :],
                                     op=ALU.add),
                                 wait=["se", (("st0", "st1")[fc % 2],
                                              x2_od[fc % 2])])
                            p.op("sync", lambda s, fc=fc, tx=tx: s.dma_start(
                                x2v[:, fc, :], tx[:, :]),
                                inc=("st0", "st1")[fc % 2],
                                inc_n=16, wait=["ve"])
                            x2_od[fc % 2] = p.cnt(("st0", "st1")[fc % 2])
                            p.op("tensor", lambda t, fc=fc, tx=tx: t.matmul(
                                ps4_s1[:, :], ones_f[:, :], tx[:, :],
                                start=(fc == 0), stop=(fc == 15)), wait=["ve"])
                            p.op("scalar", lambda s, fc=fc, tx=tx: s.activation(
                                sq4[:, :], tx[:, :], ACTF.Square), wait=["pe"])
                            p.op("tensor", lambda t, fc=fc: t.matmul(
                                ps4_s2[:, :], ones_bf[:, :], sq4[:, :],
                                start=(fc == 0), stop=(fc == 15)), wait=["se"])

                ln_helper(ps4_s1, ps4_s2, ps4_m, ps4_r, mb4, rb4, FT)
                st_od = [p.cnt("st0"), p.cnt("st1")]
                x2done = [("st0", p.cnt("st0")), ("st1", p.cnt("st1"))]
                for cc in range(16):
                    tx = t4a if cc % 2 == 0 else t4b
                    p.op("sync", lambda sy, cc=cc: sy.dma_start(
                        osub[:, :], x2v[:, cc, :]), inc="xdma", inc_n=16,
                        wait=([("ve",)] + (x2done if cc == 0 else [])))
                    p.op("vector", lambda v, cc=cc, tx=tx: v.tensor_tensor(
                        tx[:, :], osub[:, :], mb4[:, :], op=ALU.subtract),
                        wait=["xdma", "ve",
                              (("st0", "st1")[cc % 2], st_od[cc % 2])])
                    p.op("vector", lambda v, cc=cc, tx=tx: v.tensor_tensor(
                        tx[:, :], tx[:, :], rb4[:, :], op=ALU.mult),
                        wait=["ve"])
                    p.op("vector", lambda v, cc=cc, tx=tx: v.tensor_tensor(
                        tx[:, :], tx[:, :], y1t[:, cc, :], op=ALU.add),
                        wait=["ve"])
                    p.op("sync", lambda s, cc=cc, fsl=fsl, tx=tx: s.dma_start(
                        outv[:, cc, fsl], tx[:, :]),
                        inc=("st0", "st1")[cc % 2], inc_n=16,
                        wait=["ve"])
                    st_od[cc % 2] = p.cnt(("st0", "st1")[cc % 2])
                    p.keepwarm(("ve",))
                if ft + 1 < NFT:
                    nfsl = slice((ft + 1) * FT, (ft + 2) * FT)
                    p.op("sync", lambda sy, nfsl=nfsl: sy.dma_start(
                        y1t[:], y1v[:, :, nfsl]), inc="xdma", inc_n=16,
                        wait=[("ve",), ("odma",)])
                    y1_x[0] = p.cnt("xdma")
            p.flush()

        p.barrier(("odma", "st0", "st1"), engines=("sync",))
        p.flush()

    return nc



def _perm_pi():
    perm = []
    for c2 in range(8):
        perm += list(range(128 * (2 * c2), 128 * (2 * c2) + 64))
        perm += list(range(128 * (2 * c2 + 1), 128 * (2 * c2 + 1) + 64))
    for c2 in range(8):
        perm += list(range(128 * (2 * c2) + 64, 128 * (2 * c2) + 128))
        perm += list(range(128 * (2 * c2 + 1) + 64, 128 * (2 * c2 + 1) + 128))
    return np.array(perm)


def _wproj_p(wT):
    return np.ascontiguousarray(
        wT.reshape(16, 128, 16, 128).transpose(1, 2, 0, 3))


def _rope_tables(positions):
    half = 64
    theta = 10000.0 ** (-np.arange(half, dtype=np.float64) / half)
    ang = np.tile(theta, 2)[:, None] * positions[None, :].astype(np.float64)
    out = []
    for a in (np.cos(ang), np.sin(ang)):
        hi = a.astype(np.float32)
        lo = (a - hi.astype(np.float64)).astype(np.float32)
        out += [hi, lo]
    return out


def stage_inputs(inputs):
    import ml_dtypes
    f32 = np.float32
    x = np.asarray(inputs["L0_attn_x"], dtype=f32)
    Wq = np.asarray(inputs["L0_attn_Wq"], dtype=f32)
    Wk = np.asarray(inputs["L0_attn_Wk"], dtype=f32)
    Wv = np.asarray(inputs["L0_attn_Wv"], dtype=f32)
    Wo = np.asarray(inputs["L0_attn_Wo"], dtype=f32)
    Wg = np.asarray(inputs["L0_ffn_Wg"], dtype=f32)
    bg = np.asarray(inputs["L0_ffn_bg"], dtype=f32)
    Wup = np.asarray(inputs["L0_ffn_Wup"], dtype=f32)
    bup = np.asarray(inputs["L0_ffn_bup"], dtype=f32)
    Wsw = np.asarray(inputs["L0_ffn_Wsw"], dtype=f32)
    bsw = np.asarray(inputs["L0_ffn_bsw"], dtype=f32)
    Wdn = np.asarray(inputs["L0_ffn_Wdn"], dtype=f32)
    bdn = np.asarray(inputs["L0_ffn_bdn"], dtype=f32)
    Wsu = np.asarray(inputs["L0_ffn_Wsu"], dtype=f32)
    bsu = np.asarray(inputs["L0_ffn_bsu"], dtype=f32)
    Wsa = np.asarray(inputs["L0_ffn_Wsa"], dtype=f32)
    bsa = np.asarray(inputs["L0_ffn_bsa"], dtype=f32)
    Wsd = np.asarray(inputs["L0_ffn_Wsd"], dtype=f32)
    bsd = np.asarray(inputs["L0_ffn_bsd"], dtype=f32)

    perm = _perm_pi()
    shared = {
        "wqTp": _wproj_p(np.ascontiguousarray(Wq.T)[:, perm]),
        "wkTp": _wproj_p(np.ascontiguousarray(Wk.T)[:, perm]),
        "wvTp": np.ascontiguousarray(
            np.ascontiguousarray(Wv.T).reshape(16, 128, 4, 512)
            .transpose(1, 2, 0, 3)),
        "woTp": _wproj_p(np.ascontiguousarray(Wo.T)),
        "ident": np.eye(128, dtype=ml_dtypes.bfloat16),
        "ident32": np.eye(128, dtype=f32),
        "wgp": np.ascontiguousarray(Wg.reshape(2, 16, 128, 8).transpose(0, 2, 1, 3)),
        "bgt": np.ascontiguousarray(bg.T),
        "wupp": np.ascontiguousarray(
            Wup.reshape(NEXP, 16, 128, 8, 128).transpose(0, 2, 3, 1, 4)),
        "bupt": np.ascontiguousarray(
            bup.reshape(NEXP, 8, 128).transpose(2, 0, 1).reshape(128, NEXP * 8)),
        "wswp": np.ascontiguousarray(
            Wsw.reshape(NEXP, 8, 128, 8, 128).transpose(0, 2, 3, 1, 4)),
        "bswt": np.ascontiguousarray(
            bsw.reshape(NEXP, 8, 128).transpose(2, 0, 1).reshape(128, NEXP * 8)),
        "wdnp": np.ascontiguousarray(
            Wdn.reshape(NEXP, 8, 128, 16, 128).transpose(3, 2, 0, 1, 4)),
        "bdn": bdn,
        "wsup": np.ascontiguousarray(
            Wsu.reshape(16, 128, 16, 128).transpose(1, 2, 0, 3)),
        "bsut": np.ascontiguousarray(bsu.reshape(16, 128).T),
        "wsap": np.ascontiguousarray(
            Wsa.reshape(8, 128, 8, 128).transpose(1, 2, 0, 3)),
        "bsat": np.ascontiguousarray(bsa.reshape(8, 128).T),
        "wsdp": np.ascontiguousarray(
            Wsd.reshape(16, 128, 16, 128).transpose(2, 1, 0, 3)),
        "bsdt": np.ascontiguousarray(bsd.reshape(16, 128).T),
    }
    chi, clo, shi, slo = _rope_tables(np.arange(NSEQ))
    shared.update(ckhi=chi, cklo=clo, skhi=shi, sklo=slo)

    in_maps = []
    pos_per_core = []
    xcT = [np.ascontiguousarray(x[b].T) for b in range(NB)]
    for c in range(8):
        b, r = c // 4, c % 4
        tiles = r + 4 * np.arange(16)
        pos = (tiles[:, None] * 128 + np.arange(128)[None, :]).reshape(-1)
        pos_per_core.append((b, pos))
        cq_hi, cq_lo, sq_hi, sq_lo = _rope_tables(pos)
        jj = np.arange(16)[None, :, None]
        kp = np.arange(128)[:, None, None]
        qq = np.arange(512)[None, None, :]
        reltile = r + 4 * (qq // 128)
        m = (jj < reltile) | ((jj == reltile) & (kp <= (qq % 128)))
        im = dict(shared)
        im.update(
            xc=xcT[b],
            xq=np.ascontiguousarray(xcT[b][:, pos]),
            cqhi=cq_hi, cqlo=cq_lo, sqhi=sq_hi, sqlo=sq_lo,
            dmask=m.astype(ml_dtypes.bfloat16),
        )
        in_maps.append(im)
    return in_maps, pos_per_core


def kernel(**inputs):
    in_maps, pos_per_core = stage_inputs(inputs)
    nc = build_nc()
    res = run_bass_kernel_spmd(nc, in_maps, core_ids=list(range(8)))
    full = np.empty((NB, NSEQ, D), dtype=np.float32)
    for c in range(8):
        b, pos = pos_per_core[c]
        full[b, pos, :] = np.asarray(res.results[c]["out"], dtype=np.float32).T
    return full


# version 11
# speedup vs baseline: 1.0739x; 1.0739x over previous
import contextlib
import numpy as np

import concourse.bass as bass
import concourse.mybir as mybir
from concourse.bass_utils import run_bass_kernel_spmd

F32 = mybir.dt.float32
BF16 = mybir.dt.bfloat16
ALU = mybir.AluOpType
AX = mybir.AxisListType
ACTF = mybir.ActivationFunctionType

D = 2048
NSEQ = 8192
NB = 2
NH = 16
NE = 8
NEXP = 6
LN_EPS = 1e-5
DEN_EPS = 1e-6
TOK = 2048
SEG = 1024
NSEG = NSEQ // SEG
FT = 512
NFT = TOK // FT
BIG = 1.0e30


class Prog:

    ENGS = ("tensor", "vector", "scalar", "gpsimd", "sync")
    AUTO = {"tensor": "pe", "vector": "ve", "scalar": "se"}

    def __init__(self, nc, block, stack):
        self.nc = nc
        self.block = block
        self.stack = stack
        self.sems = {}
        self.q = {e: [] for e in self.ENGS}

    def sem(self, *names):
        for n in names:
            self.sems[n] = [self.stack.enter_context(self.nc.semaphore(n)), 0]

    def cnt(self, name):
        return self.sems[name][1]

    def _waits(self, wait):
        waits = []
        for w in (wait or ()):
            if isinstance(w, tuple):
                n = w[0]
                v = w[1] if len(w) > 1 else None
            else:
                n, v = w, None
            waits.append((self.sems[n][0], self.cnt(n) if v is None else v))
        return tuple(waits)

    def op(self, eng, make, inc=None, inc_n=1, wait=None):
        waits = self._waits(wait)
        if inc is None and eng in self.AUTO:
            inc = self.AUTO[eng]
            inc_n = 1
        upd = None
        if inc is not None:
            s = self.sems[inc]
            s[1] += inc_n
            upd = (s[0], inc_n)

        def emit(e, make=make, waits=waits, upd=upd):
            for h, v in waits:
                if v > 0:
                    e.wait_ge(h, v)
            inst = make(e)
            if upd is not None:
                inst.then_inc(upd[0], upd[1])

        self.q[eng].append(emit)

    def wait(self, eng, *names_vals):
        waits = self._waits(names_vals)

        def emit(e, waits=waits):
            for h, v in waits:
                if v > 0:
                    e.wait_ge(h, v)

        self.q[eng].append(emit)

    def flush(self):
        for name in self.ENGS:
            fs = self.q[name]
            if not fs:
                continue
            self.q[name] = []
            getattr(self.block, name)(lambda e, fs=tuple(fs): [f(e) for f in fs])

    def barrier(self, sems, engines=ENGS):
        for eng in engines:
            self.wait(eng, *sems)

    def set_warm(self, psum_ap, lhs_ap, rhs_ap):
        self._warm = (psum_ap, lhs_ap, rhs_ap)

    def keepwarm(self, *wait):
        ps, lh, rh = self._warm
        self.op("tensor", lambda t: t.matmul(ps, lh, rh, start=True, stop=True),
                wait=list(wait))


def build_nc(debug=False):
    nc = bass.Bass("TRN2", target_bir_lowering=False, debug=False)
    skind = "ExternalOutput" if debug else "Internal"

    dt = nc.dram_tensor
    xc = dt("xc", [D, NSEQ], F32, kind="ExternalInput").ap()
    xq = dt("xq", [D, TOK], F32, kind="ExternalInput").ap()
    wqTp = dt("wqTp", [128, 16, 16, 128], F32, kind="ExternalInput").ap()
    wkTp = dt("wkTp", [128, 16, 16, 128], F32, kind="ExternalInput").ap()
    wvTp = dt("wvTp", [128, 4, 16, 512], F32, kind="ExternalInput").ap()
    woTp = dt("woTp", [128, 16, 16, 128], F32, kind="ExternalInput").ap()
    ckhi = dt("ckhi", [128, NSEQ], F32, kind="ExternalInput").ap()
    cklo = dt("cklo", [128, NSEQ], F32, kind="ExternalInput").ap()
    skhi = dt("skhi", [128, NSEQ], F32, kind="ExternalInput").ap()
    sklo = dt("sklo", [128, NSEQ], F32, kind="ExternalInput").ap()
    cqhi = dt("cqhi", [128, TOK], F32, kind="ExternalInput").ap()
    cqlo = dt("cqlo", [128, TOK], F32, kind="ExternalInput").ap()
    sqhi = dt("sqhi", [128, TOK], F32, kind="ExternalInput").ap()
    sqlo = dt("sqlo", [128, TOK], F32, kind="ExternalInput").ap()
    ident_in = dt("ident", [128, 128], BF16, kind="ExternalInput").ap()
    ident32_in = dt("ident32", [128, 128], F32, kind="ExternalInput").ap()
    dmask_in = dt("dmask", [128, 16, 512], BF16, kind="ExternalInput").ap()

    wgp = dt("wgp", [2, 128, 16, 8], F32, kind="ExternalInput").ap()
    bgt = dt("bgt", [8, 2], F32, kind="ExternalInput").ap()
    wupp = dt("wupp", [NEXP, 128, 8, 16, 128], F32, kind="ExternalInput").ap()
    bupt = dt("bupt", [128, NEXP * 8], F32, kind="ExternalInput").ap()
    wswp = dt("wswp", [NEXP, 128, 8, 8, 128], F32, kind="ExternalInput").ap()
    bswt = dt("bswt", [128, NEXP * 8], F32, kind="ExternalInput").ap()
    wdnp = dt("wdnp", [16, 128, NEXP, 8, 128], F32, kind="ExternalInput").ap()
    bdn_in = dt("bdn", [NEXP, D], F32, kind="ExternalInput").ap()
    wsup = dt("wsup", [128, 16, 16, 128], F32, kind="ExternalInput").ap()
    bsut = dt("bsut", [128, 16], F32, kind="ExternalInput").ap()
    wsap = dt("wsap", [128, 8, 8, 128], F32, kind="ExternalInput").ap()
    bsat = dt("bsat", [128, 8], F32, kind="ExternalInput").ap()
    wsdp = dt("wsdp", [16, 128, 16, 128], F32, kind="ExternalInput").ap()
    bsdt = dt("bsdt", [128, 16], F32, kind="ExternalInput").ap()

    out = dt("out", [D, TOK], F32, kind="ExternalOutput").ap()

    Ks = dt("Ks", [16, 128, NSEQ], BF16, kind=skind).ap()
    Vs = dt("Vs", [16, 128, 64, 128], BF16, kind=skind).ap()
    Qs = dt("Qs", [16, 128, TOK], BF16, kind=skind).ap()
    y1s = dt("y1s", [D, TOK], F32, kind=skind).ap()
    xt2s = dt("xt2s", [D, FT], F32, kind="Internal").ap()
    cosk = dt("cosk", [128, NSEQ], F32, kind="Internal").ap()
    sink = dt("sink", [128, NSEQ], F32, kind="Internal").ap()
    cosq = dt("cosq", [128, TOK], F32, kind="Internal").ap()
    sinq = dt("sinq", [128, TOK], F32, kind="Internal").ap()

    wqTb = dt("wqTb", [128, 16, 16, 128], BF16, kind="Internal").ap()
    wkTb = dt("wkTb", [128, 16, 16, 128], BF16, kind="Internal").ap()
    wvTb = dt("wvTb", [128, 4, 16, 512], BF16, kind="Internal").ap()
    woTb = dt("woTb", [128, 16, 16, 128], BF16, kind="Internal").ap()
    wgb = dt("wgb", [2, 128, 16, 8], BF16, kind="Internal").ap()
    wupb = dt("wupb", [NEXP, 128, 8, 16, 128], BF16, kind="Internal").ap()
    wswb = dt("wswb", [NEXP, 128, 8, 8, 128], BF16, kind="Internal").ap()
    wdnb = dt("wdnb", [16, 128, NEXP, 8, 128], BF16, kind="Internal").ap()
    bdnb = dt("bdnb", [NEXP, D], BF16, kind="Internal").ap()
    wsub = dt("wsub", [128, 16, 16, 128], BF16, kind="Internal").ap()
    wsab = dt("wsab", [128, 8, 8, 128], BF16, kind="Internal").ap()
    wsdb = dt("wsdb", [16, 128, 16, 128], BF16, kind="Internal").ap()

    xcv = xc.rearrange("(c p) n -> p c n", p=128)
    xqv = xq.rearrange("(c p) n -> p c n", p=128)
    y1v = y1s.rearrange("(c p) n -> p c n", p=128)
    x2v = xt2s.rearrange("(c p) n -> p c n", p=128)
    outv = out.rearrange("(c p) n -> p c n", p=128)

    with contextlib.ExitStack() as stack:
        ctx = stack.enter_context
        block = ctx(nc.Block())
        p = Prog(nc, block, stack)
        p.sem("xdma", "wdma", "odma", "cdma", "c2d0", "c2d1",
              "la0", "la1", "lb0", "lb1", "lh0", "lh1", "st0", "st1",
              "pe", "ve", "se", "gp")

        ones_bf = ctx(nc.sbuf_tensor("ones_bf", [128, 1], BF16))
        ones_f = ctx(nc.sbuf_tensor("ones_f", [128, 1], F32))
        ones1_f = ctx(nc.sbuf_tensor("ones1_f", [1, 128], F32))
        ones1_bf = ctx(nc.sbuf_tensor("ones1_bf", [1, 128], BF16))
        ident = ctx(nc.sbuf_tensor("ident_sb", [128, 128], BF16))
        ident32 = ctx(nc.sbuf_tensor("ident32_sb", [128, 128], F32))
        stat = ctx(nc.sbuf_tensor("stat", [1, 512], F32))
        stat2 = ctx(nc.sbuf_tensor("stat2", [1, 512], F32))
        stat3 = ctx(nc.sbuf_tensor("stat3", [1, 512], F32))
        epsb = ctx(nc.sbuf_tensor("epsb", [1, 1], F32))

        p.op("vector", lambda v: v.memset(ones_bf[:], 1.0))
        p.op("vector", lambda v: v.memset(ones_f[:], 1.0))
        p.op("vector", lambda v: v.memset(ones1_f[:], 1.0))
        p.op("vector", lambda v: v.memset(ones1_bf[:], 1.0))
        p.op("vector", lambda v: v.memset(epsb[:], float(LN_EPS)))
        p.op("sync", lambda s: s.dma_start(ident[:], ident_in), inc="xdma", inc_n=16)
        p.op("sync", lambda s: s.dma_start(ident32[:], ident32_in), inc="xdma",
             inc_n=16)

        def ln_helper(ps_s1, ps_s2, ps_m, ps_r, mb_, rb_, w, soff=0):
            ssl = slice(soff, soff + w)
            p.op("vector", lambda v: v.tensor_scalar(
                stat[0:1, 0:w], ps_s1[0:1, 0:w], 1.0 / D, 0.0,
                op0=ALU.mult, op1=ALU.add), wait=["pe"])
            p.op("vector", lambda v: v.tensor_tensor(
                stat2[0:1, 0:w], stat[0:1, 0:w], stat[0:1, 0:w], op=ALU.mult),
                wait=["ve"])
            p.op("vector", lambda v: v.scalar_tensor_tensor(
                stat3[0:1, 0:w], ps_s2[0:1, 0:w], 1.0 / D, stat2[0:1, 0:w],
                op0=ALU.mult, op1=ALU.subtract), wait=["ve"])
            p.op("scalar", lambda s: s.activation(
                stat2[0:1, 0:w], stat3[0:1, 0:w], ACTF.Sqrt, bias=epsb[0:1, 0:1]),
                wait=["ve"])
            p.op("vector", lambda v: v.reciprocal(stat3[0:1, 0:w], stat2[0:1, 0:w]),
                 wait=["se"])
            p.op("tensor", lambda t: t.matmul(
                ps_m[:, 0:w], ones1_f[:, :], stat[0:1, 0:w], start=True, stop=True),
                wait=["ve"])
            p.op("tensor", lambda t: t.matmul(
                ps_r[:, 0:w], ones1_f[:, :], stat3[0:1, 0:w], start=True, stop=True))
            p.op("vector", lambda v: v.tensor_copy(mb_[:, ssl], ps_m[:, 0:w]),
                 wait=["pe"])
            p.op("vector", lambda v: v.tensor_copy(rb_[:, ssl], ps_r[:, 0:w]))

        with contextlib.ExitStack() as ph0:
            c0 = ph0.enter_context
            rwork = c0(nc.sbuf_tensor("rwork", [128, NSEQ], F32))
            rwork2 = c0(nc.sbuf_tensor("rwork2", [128, NSEQ], F32))
            for hi, lo, dst, n in ((ckhi, cklo, cosk, NSEQ),
                                   (skhi, sklo, sink, NSEQ),
                                   (cqhi, cqlo, cosq, TOK),
                                   (sqhi, sqlo, sinq, TOK)):
                p.op("sync", lambda s, hi=hi, n=n: s.dma_start(rwork[:, 0:n], hi),
                     inc="xdma", inc_n=16, wait=["odma"])
                p.op("sync", lambda s, lo=lo, n=n: s.dma_start(rwork2[:, 0:n], lo),
                     inc="xdma", inc_n=16)
                p.op("vector", lambda v, n=n: v.tensor_tensor(
                    rwork[:, 0:n], rwork[:, 0:n], rwork2[:, 0:n], op=ALU.add),
                    wait=["xdma"])
                p.op("sync", lambda s, dst=dst, n=n: s.dma_start(dst, rwork[:, 0:n]),
                     inc="odma", inc_n=16, wait=["ve"])
            rope_odma = p.cnt("odma")
            p.flush()

        def flatten(ap):
            n = len(ap.shape)
            names = " ".join(chr(ord("a") + i) for i in range(n))
            return ap.rearrange(f"{names} -> ({names})")

        with contextlib.ExitStack() as phb:
            cb = phb.enter_context
            bounce0 = cb(nc.sbuf_tensor("bounce0", [128, 2048], BF16))
            bounce1 = cb(nc.sbuf_tensor("bounce1", [128, 2048], BF16))

            cast_jobs_att = [(wkTp, wkTb), (wqTp, wqTb), (wvTp, wvTb), (woTp, woTb)]
            cast_jobs_ffn = ([(wgp, wgb), (bdn_in, bdnb), (wsup, wsub),
                              (wsap, wsab), (wsdp, wsdb)] +
                             [(wupp[n], wupb[n]) for n in range(NEXP)] +
                             [(wswp[n], wswb[n]) for n in range(NEXP)] +
                             [(wdnp[f], wdnb[f]) for f in range(16)])
            si = [0]
            first_cast = [True]

            def emit_casts(jobs):
                W0 = 2048
                for src, dst in jobs:
                    total = int(np.prod(src.shape))
                    srcf, dstf = flatten(src), flatten(dst)
                    off = 0
                    while off < total:
                        nele = min(128 * W0, total - off)
                        if nele >= W0:
                            nele -= nele % W0
                            shp = (nele // W0, W0)
                        else:
                            shp = (1, nele)
                        i = si[0]
                        sl = (bounce0, bounce1)[i]
                        ssem = ("c2d0", "c2d1")[i]
                        p.op("gpsimd", lambda g, sl=sl, srcf=srcf, off=off, shp=shp:
                             g.dma_start(sl[0:shp[0], 0:shp[1]],
                                         srcf[off:off + shp[0] * shp[1]]
                                         .rearrange("(a b) -> a b", b=shp[1])),
                             inc="cdma", inc_n=16,
                             wait=([(ssem, p.cnt(ssem))] +
                                   ([("odma", rope_odma)] if first_cast[0]
                                    else [])))
                        first_cast[0] = False
                        p.op("gpsimd", lambda g, sl=sl, dstf=dstf, off=off, shp=shp:
                             g.dma_start(dstf[off:off + shp[0] * shp[1]]
                                         .rearrange("(a b) -> a b", b=shp[1]),
                                         sl[0:shp[0], 0:shp[1]]),
                             inc=ssem, inc_n=16, wait=["cdma"])
                        si[0] ^= 1
                        off += shp[0] * shp[1]

            cast_gate = {}
            for job in cast_jobs_att:
                emit_casts([job])
                cast_gate[job[1].name] = (("c2d0", p.cnt("c2d0")),
                                          ("c2d1", p.cnt("c2d1")))
            att_cast_c2 = (p.cnt("c2d0"), p.cnt("c2d1"))
            emit_casts(cast_jobs_ffn)
            ffn_cast_c2 = (p.cnt("c2d0"), p.cnt("c2d1"))
            p.flush()

            with contextlib.ExitStack() as ph1:
                c1 = ph1.enter_context
                xseg = c1(nc.sbuf_tensor("xseg", [128, 16, SEG], F32))
                xqseg = c1(nc.sbuf_tensor("xqseg", [128, 16, 256], F32))
                xln = c1(nc.sbuf_tensor("xln", [128, 16, SEG], BF16))
                xqln = c1(nc.sbuf_tensor("xqln", [128, 16, 256], BF16))
                sqch = c1(nc.sbuf_tensor("sqch", [128, 2, 512], BF16))
                slab_ar = c1(nc.sbuf_tensor("slab_ar", [128, 8192], BF16))
                wsl0 = slab_ar[:, 0:4096].rearrange("p (a b) -> p a b", b=256)
                wsl1 = slab_ar[:, 4096:8192].rearrange("p (a b) -> p a b", b=256)
                vslA = slab_ar[:, 0:8192].rearrange("p (a b) -> p a b", b=512)
                vsl = c1(nc.sbuf_tensor("vsl", [128, 16, 512], BF16))
                ropec = c1(nc.sbuf_tensor("ropec", [128, SEG], F32))
                ropes = c1(nc.sbuf_tensor("ropes", [128, SEG], F32))
                ropecq = c1(nc.sbuf_tensor("ropecq", [128, 256], F32))
                ropesq = c1(nc.sbuf_tensor("ropesq", [128, 256], F32))
                tt1 = c1(nc.sbuf_tensor("tt1", [128, SEG], F32))
                tt2 = c1(nc.sbuf_tensor("tt2", [128, SEG], F32))
                ka = c1(nc.sbuf_tensor("ka", [128, SEG], F32))
                kb = c1(nc.sbuf_tensor("kb", [128, SEG], F32))
                pout = c1(nc.sbuf_tensor("pout", [128, 2, SEG], BF16))
                qall = (xqseg[:].rearrange("p a b -> p (a b)")
                        .bitcast(BF16)[:, 0:4096]
                        .rearrange("p (a b) -> p a b", b=256))
                mb = c1(nc.sbuf_tensor("mb", [128, SEG], F32))
                rb = c1(nc.sbuf_tensor("rb", [128, SEG], F32))
                mbq = tt2[:, 0:256]
                rbq = tt2[:, 256:512]

                ps_s1 = c1(nc.psum_tensor("ps_s1", [1, 512], F32))
                ps_s2 = c1(nc.psum_tensor("ps_s2", [1, 512], F32))
                ps_m = c1(nc.psum_tensor("ps_m", [128, 512], F32))
                ps_r = c1(nc.psum_tensor("ps_r", [128, 512], F32))
                ps_a = c1(nc.psum_tensor("ps_a", [128, SEG], F32))
                ps_b = c1(nc.psum_tensor("ps_b", [128, SEG], F32))
                p.set_warm(ps_s1[0:1, 0:1], ones_bf[0:1, 0:1], ones_bf[0:1, 0:1])

                def ln_block(src, n_t, dst, mb_, rb_, ve_gate=None,
                             xgate=None):
                    for sub in range((n_t + 511) // 512):
                        w = min(512, n_t - sub * 512)
                        sl = slice(sub * 512, sub * 512 + w)
                        sq_prev = [0, 0]
                        for cc in range(16):
                            wv = None
                            if cc == 0:
                                wv = (["ve", "se"] if ve_gate is None
                                      else [("ve", ve_gate), ("se",)])
                                if xgate is not None:
                                    wv.append(xgate[sub])
                            p.op("tensor", lambda t, cc=cc, sl=sl, w=w: t.matmul(
                                ps_s1[0:1, 0:w], ones_f[:, :], src[:, cc, sl],
                                start=(cc == 0), stop=(cc == 15)),
                                wait=wv)
                            p.op("scalar", lambda s, cc=cc, sl=sl, w=w:
                                 s.activation(
                                     sqch[:, cc % 2, 0:w], src[:, cc, sl],
                                     ACTF.Square),
                                 wait=([("pe",)] if cc == 0 else
                                       [("pe", sq_prev[cc % 2])]))
                            p.op("tensor", lambda t, cc=cc, w=w: t.matmul(
                                ps_s2[0:1, 0:w], ones_bf[:, :],
                                sqch[:, cc % 2, 0:w],
                                start=(cc == 0), stop=(cc == 15)), wait=["se"])
                            sq_prev[cc % 2] = p.cnt("pe")
                        ln_helper(ps_s1, ps_s2, ps_m, ps_r, mb_, rb_, w,
                                  soff=sub * 512)
                    helper_ve = p.cnt("ve")
                    chunk_ve = []
                    for cc in range(16):
                        p.op("vector", lambda v, cc=cc: v.tensor_tensor(
                            tt1[:, 0:n_t], src[:, cc, 0:n_t], mb_[:, 0:n_t],
                            op=ALU.subtract), wait=["ve"])
                        p.op("vector", lambda v, cc=cc: v.tensor_tensor(
                            dst[:, cc, 0:n_t], tt1[:, 0:n_t], rb_[:, 0:n_t],
                            op=ALU.mult), wait=["ve"])
                        chunk_ve.append(p.cnt("ve"))
                        if cc % 4 == 3:
                            p.keepwarm(("ve",))
                    return chunk_ve, helper_ve

                prev_pref = None
                for s in range(NSEG):
                    tsl = slice(s * SEG, (s + 1) * SEG)
                    qsl = slice(s * 256, (s + 1) * 256)
                    p.barrier(("pe", "ve", "se", "odma"), engines=("sync",))
                    if prev_pref is None:
                        p.op("sync", lambda sy, tsl=tsl: sy.dma_start(
                            xseg[:, :, 0:512],
                            xcv[:, :, tsl.start:tsl.start + 512]),
                            inc="lh0", inc_n=16)
                        xh0 = ("lh0", p.cnt("lh0"))
                        p.op("sync", lambda sy, tsl=tsl: sy.dma_start(
                            xseg[:, :, 512:1024],
                            xcv[:, :, tsl.start + 512:tsl.stop]),
                            inc="lh1", inc_n=16)
                        xh1 = ("lh1", p.cnt("lh1"))
                        p.op("sync", lambda sy, qsl=qsl: sy.dma_start(
                            xqseg[:], xqv[:, :, qsl]), inc="xdma", inc_n=16)
                    else:
                        xh0, xh1 = prev_pref[0], prev_pref[1]
                    p.op("sync", lambda sy, tsl=tsl: sy.dma_start(
                        ropec[:], cosk[:, tsl]), inc="xdma", inc_n=16,
                        wait=[("odma", rope_odma)])
                    p.op("sync", lambda sy, tsl=tsl: sy.dma_start(
                        ropes[:], sink[:, tsl]), inc="xdma", inc_n=16)
                    p.op("sync", lambda sy, qsl=qsl: sy.dma_start(
                        ropecq[:], cosq[:, qsl]), inc="xdma", inc_n=16)
                    p.op("sync", lambda sy, qsl=qsl: sy.dma_start(
                        ropesq[:], sinq[:, qsl]), inc="xdma", inc_n=16)
                    p.barrier(("xdma",), engines=("vector", "scalar"))
                    xln_ve, h_ve = ln_block(xseg, SEG, xln, mb, rb,
                                            xgate=(xh0, xh1))
                    xqln_ve, _ = ln_block(xqseg, 256, xqln, mbq, rbq,
                                          ve_gate=h_ve)
                    xln_ve = xln_ve

                    def proj_rope(wsrc, src_ln, n_t, rc, rs, dst_hbm, oslice,
                                  first, chunk_ve=None, dst_sb=None):
                        bufs = (wsl0, wsl1)
                        sems = ("la0", "la1")
                        lcnt = [p.cnt("la0"), p.cnt("la1")]
                        for oc in range(8):
                            bi = oc % 2
                            if oc == 0:
                                wv = (list(cast_gate[wsrc.name]) if first
                                      else [])
                                p.op("sync", lambda sy, wsrc=wsrc: sy.dma_start(
                                    bufs[0][:, :, 0:128], wsrc[:, 0, :, :]),
                                    inc=sems[0], inc_n=16,
                                    wait=wv + [("pe",)])
                                p.op("sync", lambda sy, wsrc=wsrc: sy.dma_start(
                                    bufs[0][:, :, 128:256], wsrc[:, 8, :, :]),
                                    inc=sems[0], inc_n=16)
                                lcnt[0] = p.cnt(sems[0])
                            if oc + 1 < 8:
                                nb = (oc + 1) % 2
                                p.op("sync", lambda sy, oc=oc, nb=nb, wsrc=wsrc:
                                     sy.dma_start(bufs[nb][:, :, 0:128],
                                                  wsrc[:, oc + 1, :, :]),
                                     inc=sems[nb], inc_n=16, wait=[("pe",)])
                                p.op("sync", lambda sy, oc=oc, nb=nb, wsrc=wsrc:
                                     sy.dma_start(bufs[nb][:, :, 128:256],
                                                  wsrc[:, oc + 9, :, :]),
                                     inc=sems[nb], inc_n=16)
                                lcnt[nb] = p.cnt(sems[nb])
                            p.wait("tensor", (sems[bi], lcnt[bi]))
                            for half, ps in ((0, ps_a), (1, ps_b)):
                                for sub in range((n_t + 511) // 512):
                                    w = min(512, n_t - sub * 512)
                                    sl = slice(sub * 512, sub * 512 + w)
                                    for kc in range(16):
                                        wv = None
                                        if oc == 0 and chunk_ve is not None:
                                            wv = [("ve", chunk_ve[kc])]
                                            if kc == 0 and half == 0:
                                                wv.append(("se",))
                                        elif kc == 0 and half == 0:
                                            wv = ["se"]
                                        p.op("tensor", lambda t, kc=kc, sl=sl,
                                             half=half, ps=ps, bi=bi: t.matmul(
                                                 ps[:, sl],
                                                 bufs[bi][:, kc,
                                                          128 * half:
                                                          128 * half + 128],
                                                 src_ln[:, kc, sl],
                                                 start=(kc == 0),
                                                 stop=(kc == 15)),
                                             wait=wv)
                            p.op("scalar", lambda s, n_t=n_t: s.copy(
                                ka[:, 0:n_t], ps_a[:, 0:n_t]),
                                wait=["pe", ("ve",)])
                            p.op("scalar", lambda s, n_t=n_t: s.copy(
                                kb[:, 0:n_t], ps_b[:, 0:n_t]))
                            n_ = n_t
                            p.op("vector", lambda v: v.tensor_tensor(
                                tt1[:, 0:n_], ka[:, 0:n_], rc[:, 0:n_],
                                op=ALU.mult), wait=[("odma",), "se", "ve"])
                            p.op("vector", lambda v: v.tensor_tensor(
                                tt2[:, 0:n_], kb[:, 0:n_], rs[:, 0:n_],
                                op=ALU.mult))
                            po0 = (pout[:, 0, 0:n_] if dst_sb is None
                                   else dst_sb[:, oc, :])
                            po1 = (pout[:, 1, 0:n_] if dst_sb is None
                                   else dst_sb[:, oc + 8, :])
                            p.op("vector", lambda v, po0=po0: v.tensor_tensor(
                                po0, tt1[:, 0:n_], tt2[:, 0:n_],
                                op=ALU.subtract), wait=["ve"])
                            p.op("vector", lambda v: v.tensor_tensor(
                                tt1[:, 0:n_], ka[:, 0:n_], rs[:, 0:n_],
                                op=ALU.mult), wait=["ve"])
                            p.op("vector", lambda v: v.tensor_tensor(
                                tt2[:, 0:n_], kb[:, 0:n_], rc[:, 0:n_],
                                op=ALU.mult))
                            p.op("vector", lambda v, po1=po1: v.tensor_tensor(
                                po1, tt1[:, 0:n_], tt2[:, 0:n_],
                                op=ALU.add), wait=["ve"])
                            if dst_sb is None:
                                p.op("scalar", lambda sc_, oc=oc: sc_.dma_start(
                                    dst_hbm[oc, :, oslice], pout[:, 0, 0:n_]),
                                    inc="odma", inc_n=16, wait=["ve"])
                                p.op("scalar", lambda sc_, oc=oc: sc_.dma_start(
                                    dst_hbm[oc + 8, :, oslice],
                                    pout[:, 1, 0:n_]),
                                    inc="odma", inc_n=16)
                        if dst_sb is not None:
                            p.op("scalar", lambda sc_: sc_.dma_start(
                                dst_hbm[:, :, oslice]
                                .rearrange("a p b -> p a b"), dst_sb),
                                inc="odma", inc_n=16, wait=["ve"])

                    if s + 1 < NSEG:
                        nt_ = slice((s + 1) * SEG, (s + 2) * SEG)
                        nq_ = slice((s + 1) * 256, (s + 2) * 256)
                        p.op("sync", lambda sy, nt_=nt_: sy.dma_start(
                            xseg[:, :, 0:512],
                            xcv[:, :, nt_.start:nt_.start + 512]),
                            inc="lh0", inc_n=16, wait=[("ve",)])
                        nxh0 = ("lh0", p.cnt("lh0"))
                        p.op("sync", lambda sy, nt_=nt_: sy.dma_start(
                            xseg[:, :, 512:1024],
                            xcv[:, :, nt_.start + 512:nt_.stop]),
                            inc="lh1", inc_n=16)
                        nxh1 = ("lh1", p.cnt("lh1"))
                        nxt_pref = (nxh0, nxh1, nq_)
                    else:
                        nxt_pref = None
                    proj_rope(wkTb, xln, SEG, ropec, ropes, Ks, tsl, s == 0, xln_ve)
                    proj_rope(wqTb, xqln, 256, ropecq, ropesq, Qs, qsl,
                              s == 0, xqln_ve, dst_sb=qall)
                    if nxt_pref is not None:
                        p.op("sync", lambda sy, nq_=nxt_pref[2]: sy.dma_start(
                            xqseg[:], xqv[:, :, nq_]), inc="xdma", inc_n=16,
                            wait=[("odma",)])
                    prev_pref = nxt_pref

                    vbufs = (vsl, vslA)
                    vsem = ("lb0", "lb1")
                    vcnt = [p.cnt("lb0"), p.cnt("lb1")]

                    def load_v(dvs, bi):
                        gat = (list(cast_gate["wvTb"]) if s == 0 and dvs < 2
                               else [])
                        p.op("sync", lambda sy, dvs=dvs, bi=bi: sy.dma_start(
                            vbufs[bi][:, :, :] if bi == 0 else vbufs[bi],
                            wvTb[:, dvs, :, :]), inc=vsem[bi], inc_n=16,
                            wait=[("pe",)] + gat)
                        vcnt[bi] = p.cnt(vsem[bi])

                    load_v(0, 0)
                    ev_se = [p.cnt("se"), p.cnt("se")]
                    for dvs in range(4):
                        bi = dvs % 2
                        if dvs + 1 < 4:
                            load_v(dvs + 1, bi ^ 1)
                        p.wait("tensor", (vsem[bi], vcnt[bi]))
                        for tb in range(8):
                            half = tb % 2
                            psl = slice(half * 512, half * 512 + 512)
                            for kc in range(16):
                                p.op("tensor", lambda t, tb=tb, kc=kc, psl=psl,
                                     bi=bi: t.matmul(
                                         ps_a[:, psl],
                                         xln[:, kc, tb * 128:tb * 128 + 128],
                                         vbufs[bi][:, kc, :],
                                         start=(kc == 0), stop=(kc == 15)),
                                     wait=([("se", ev_se[half])]
                                           if kc == 0 else None))
                            p.op("scalar", lambda s, psl=psl: s.copy(
                                pout[:, 0, psl], ps_a[:, psl]),
                                wait=["pe", ("odma",)])
                            ev_se[half] = p.cnt("se")
                            p.op("scalar", lambda s, tb=tb, dvs=dvs, psl=psl,
                                 kt=s * 8 + tb: s.dma_start(
                                     Vs[4 * dvs:4 * dvs + 4, :, kt, :]
                                     .rearrange("h p d -> p h d"),
                                     pout[:, 0, psl]
                                     .rearrange("p (h d) -> p h d", d=128)),
                                inc="odma", inc_n=16, wait=["se"])
                    p.flush()

            p.barrier(("xdma", "wdma", "odma", "la0", "la1", "lb0", "lb1",
                       "pe", "ve", "se"))
            p.flush()

            with contextlib.ExitStack() as ph23:
                c23 = ph23.enter_context
                prewo = c23(nc.sbuf_tensor("prewo", [128, 16, TOK], BF16))
                dmask = c23(nc.sbuf_tensor("dmask_sb", [128, 16, 512], BF16))
                p.op("sync", lambda s: s.dma_start(dmask[:], dmask_in),
                     inc="xdma", inc_n=16)
                dmask_x = p.cnt("xdma")

                with contextlib.ExitStack() as ph2:
                    c2 = ph2.enter_context
                    khB = [c2(nc.sbuf_tensor(f"kh{i}", [128, NSEQ], BF16))
                           for i in range(2)]
                    qhB = [c2(nc.sbuf_tensor(f"qh{i}", [128, TOK], BF16))
                           for i in range(2)]
                    vtB = [c2(nc.sbuf_tensor(f"vt{i}", [128, 64, 128], BF16))
                           for i in range(2)]
                    s2sb = c2(nc.sbuf_tensor("s2sb", [128, 2048], BF16))
                    s4sb = c2(nc.sbuf_tensor("s4sb", [128, 2048], BF16))
                    rbc = c2(nc.sbuf_tensor("rbc", [128, 512], F32))

                    ps_qkA = c2(nc.psum_tensor("ps_qkA", [128, 1024], F32))
                    ps_qkB = c2(nc.psum_tensor("ps_qkB", [128, 1024], F32))
                    ps_av = c2(nc.psum_tensor("ps_av", [128, 512], F32))
                    ps_den = c2(nc.psum_tensor("ps_den", [1, 512], F32))
                    ps_rbc = c2(nc.psum_tensor("ps_rbc", [128, 512], F32))

                    lh_sem = ("lh0", "lh1")
                    lh_cnt = [0, 0]

                    def load_head(h, bi, wv):
                        hc, off = h // 2, 64 * (h % 2)
                        sm = lh_sem[bi]
                        p.op("sync", lambda sy, hc=hc, off=off, bi=bi: sy.dma_start(
                            khB[bi][0:64, :], Ks[hc, off:off + 64, :]),
                            inc=sm, inc_n=16, wait=wv)
                        p.op("sync", lambda sy, hc=hc, off=off, bi=bi: sy.dma_start(
                            khB[bi][64:128, :], Ks[8 + hc, off:off + 64, :]),
                            inc=sm, inc_n=16)
                        p.op("sync", lambda sy, hc=hc, off=off, bi=bi: sy.dma_start(
                            qhB[bi][0:64, :], Qs[hc, off:off + 64, :]),
                            inc=sm, inc_n=16)
                        p.op("sync", lambda sy, hc=hc, off=off, bi=bi: sy.dma_start(
                            qhB[bi][64:128, :], Qs[8 + hc, off:off + 64, :]),
                            inc=sm, inc_n=16)
                        p.op("sync", lambda sy, h=h, bi=bi: sy.dma_start(
                            vtB[bi][:].rearrange("p k d -> p (k d)"),
                            Vs[h].rearrange("p k d -> p (k d)")),
                            inc=sm, inc_n=16)
                        lh_cnt[bi] = p.cnt(sm)

                    load_head(0, 0, [])
                    for h in range(NH):
                        bi = h % 2
                        kh, qh, vt = khB[bi], qhB[bi], vtB[bi]
                        if h + 1 < NH:
                            load_head(h + 1, (h + 1) % 2, [("pe",)])
                        p.wait("tensor", (lh_sem[bi], lh_cnt[bi]))
                        sq_se = [0, 0]
                        s4_ve = [0, 0]
                        av_pe = [0, 0]
                        for g in range(4):
                            qg = slice(g * 512, g * 512 + 512)
                            nt2 = 8 * (g + 1)

                            def emit_av(b2, nt2=nt2, vt=vt):
                                pi = b2 % 2
                                for j2 in range(2):
                                    kt = 2 * b2 + j2
                                    off = pi * 1024 + j2 * 512
                                    p.op("tensor", lambda t, kt=kt, off=off,
                                         nt2=nt2, vt=vt: t.matmul(
                                             ps_av[:, :], vt[:, kt, :],
                                             s4sb[:, off:off + 512],
                                             start=(kt == 0),
                                             stop=(kt == nt2 * 2 - 1)),
                                         wait=([("ve", s4_ve[pi])]
                                               if j2 == 0 else None))
                                    p.op("tensor", lambda t, kt=kt, off=off,
                                         nt2=nt2: t.matmul(
                                             ps_den[:, :], ones_bf[:, :],
                                             s4sb[:, off:off + 512],
                                             start=(kt == 0),
                                             stop=(kt == nt2 * 2 - 1)))
                                av_pe[pi] = p.cnt("pe")

                            for b2 in range(nt2):
                                pi = b2 % 2
                                psqk = (ps_qkA, ps_qkB)[pi]
                                ssl = slice(pi * 1024, pi * 1024 + 1024)
                                for j2 in range(2):
                                    kt = 2 * b2 + j2
                                    p.op("tensor", lambda t, kt=kt, j2=j2,
                                         psqk=psqk, qg=qg, kh=kh, qh=qh: t.matmul(
                                             psqk[:, j2 * 512:j2 * 512 + 512],
                                             kh[:, kt * 128:kt * 128 + 128],
                                             qh[:, qg], start=True, stop=True),
                                         wait=([("se", sq_se[pi])]
                                               if j2 == 0 else None))
                                qk_pe = p.cnt("pe")
                                if b2 > 1:
                                    emit_av(b2 - 2)
                                p.op("scalar", lambda s, psqk=psqk, ssl=ssl:
                                     s.activation(
                                         s2sb[:, ssl], psqk[:, :], ACTF.Square),
                                     wait=[("pe", qk_pe), ("ve", s4_ve[pi])])
                                sq_se[pi] = p.cnt("se")
                                if b2 >= nt2 - 8:
                                    jb = (b2 - (nt2 - 8)) * 2
                                    p.op("vector", lambda v, jb=jb, ssl=ssl:
                                         v.tensor_tensor(
                                             s4sb[:, ssl], s2sb[:, ssl],
                                             dmask[:, jb:jb + 2, :]
                                             .rearrange("p a b -> p (a b)"),
                                             op=ALU.mult),
                                         wait=["se", ("pe", av_pe[pi]),
                                               ("xdma", dmask_x)])
                                    p.op("vector", lambda v, ssl=ssl:
                                         v.tensor_tensor(
                                             s4sb[:, ssl], s4sb[:, ssl],
                                             s2sb[:, ssl], op=ALU.mult),
                                         wait=["ve"])
                                else:
                                    p.op("vector", lambda v, ssl=ssl:
                                         v.tensor_tensor(
                                             s4sb[:, ssl], s2sb[:, ssl],
                                             s2sb[:, ssl], op=ALU.mult),
                                         wait=["se", ("pe", av_pe[pi])])
                                s4_ve[pi] = p.cnt("ve")
                            if nt2 > 1:
                                emit_av(nt2 - 2)
                            emit_av(nt2 - 1)
                            p.op("vector", lambda v: v.tensor_scalar(
                                stat[:, :], ps_den[:, :], float(DEN_EPS), 0.0,
                                op0=ALU.max, op1=ALU.add), wait=["pe"])
                            p.op("vector", lambda v: v.reciprocal(
                                stat2[:, :], stat[:, :]), wait=["ve"])
                            p.op("tensor", lambda t: t.matmul(
                                ps_rbc[:, :], ones1_f[:, :], stat2[:, :],
                                start=True, stop=True), wait=["ve"])
                            p.op("vector", lambda v: v.tensor_copy(
                                rbc[:, :], ps_rbc[:, :]), wait=["pe"])
                            p.op("vector", lambda v, h=h, qg=qg: v.tensor_tensor(
                                prewo[:, h, qg], ps_av[:, :], rbc[:, :],
                                op=ALU.mult), wait=["ve"])
                            for _i in range(2):
                                av_pe[_i] = max(av_pe[_i], p.cnt("pe"))
                                s4_ve[_i] = max(s4_ve[_i], p.cnt("ve"))
                    p.flush()

                with contextlib.ExitStack() as ph3:
                    c3 = ph3.enter_context
                    wosl = [c3(nc.sbuf_tensor(f"wosl{i}", [128, 16, 128], BF16))
                            for i in range(2)]
                    attn = c3(nc.sbuf_tensor("attn", [128, 16, 512], F32))
                    xqsub = c3(nc.sbuf_tensor("xqsub", [128, 16, 512], F32))
                    sq3 = c3(nc.sbuf_tensor("sq3", [128, 512], BF16))
                    mb3 = c3(nc.sbuf_tensor("mb3", [128, 512], F32))
                    rb3 = c3(nc.sbuf_tensor("rb3", [128, 512], F32))
                    t3B = [c3(nc.sbuf_tensor(f"t3a{i}", [128, 512], F32))
                           for i in range(2)]
                    ps3_o = c3(nc.psum_tensor("ps3_o", [128, 512], F32))
                    ps3_s1 = c3(nc.psum_tensor("ps3_s1", [1, 512], F32))
                    ps3_s2 = c3(nc.psum_tensor("ps3_s2", [1, 512], F32))
                    ps3_m = c3(nc.psum_tensor("ps3_m", [128, 512], F32))
                    ps3_r = c3(nc.psum_tensor("ps3_r", [128, 512], F32))

                    wo_sem = ("la0", "la1")
                    wo_cnt = [p.cnt("la0"), p.cnt("la1")]
                    first3 = [True]

                    def load_wo(oc, buf):
                        p.op("sync", lambda sy, oc=oc, buf=buf: sy.dma_start(
                            wosl[buf][:], woTb[:, oc, :, :]),
                            inc=wo_sem[buf], inc_n=16,
                            wait=([("pe",), ("ve",), ("se",)] +
                                  list(cast_gate["woTb"]) if first3[0]
                                  else [("pe",)]))
                        first3[0] = False
                        wo_cnt[buf] = p.cnt(wo_sem[buf])

                    load_wo(0, 0)
                    for sub in range(4):
                        ssl = slice(sub * 512, sub * 512 + 512)
                        p.barrier(("ve", "se", "odma", "st0", "st1"),
                                  engines=("sync",))
                        p.op("sync", lambda sy, ssl=ssl: sy.dma_start(
                            xqsub[:], xqv[:, :, ssl]), inc="xdma", inc_n=16)
                        for oc in range(16):
                            bi = oc % 2
                            if oc + 1 < 16:
                                load_wo(oc + 1, bi ^ 1)
                            elif sub + 1 < 4:
                                load_wo(0, bi ^ 1)
                            p.wait("tensor", (wo_sem[bi], wo_cnt[bi]))
                            for kc in range(16):
                                p.op("tensor", lambda t, kc=kc, ssl=ssl, bi=bi:
                                     t.matmul(
                                         ps3_o[:, :], wosl[bi][:, kc, :],
                                         prewo[:, kc, ssl],
                                         start=(kc == 0), stop=(kc == 15)),
                                     wait=(["se"] if kc == 0 else None))
                            p.op("scalar", lambda s, oc=oc: s.copy(
                                attn[:, oc, :], ps3_o[:, :]),
                                wait=(["pe", ("ve",)] if oc == 0 else ["pe"]))
                        for cc in range(16):
                            p.op("tensor", lambda t, cc=cc: t.matmul(
                                ps3_s1[:, :], ones_f[:, :], attn[:, cc, :],
                                start=(cc == 0), stop=(cc == 15)),
                                wait=(["se", "ve"] if cc == 0 else None))
                        for cc in range(16):
                            p.op("scalar", lambda s, cc=cc: s.activation(
                                sq3[:, :], attn[:, cc, :], ACTF.Square),
                                wait=["pe"])
                            p.op("tensor", lambda t, cc=cc: t.matmul(
                                ps3_s2[:, :], ones_bf[:, :], sq3[:, :],
                                start=(cc == 0), stop=(cc == 15)), wait=["se"])
                        ln_helper(ps3_s1, ps3_s2, ps3_m, ps3_r, mb3, rb3, 512)
                        p.barrier(("xdma",), engines=("vector",))
                        st_sem = ("st0", "st1")
                        st_od = [p.cnt("st0"), p.cnt("st1")]
                        for cc in range(16):
                            t3a = t3B[cc % 2]
                            p.op("vector", lambda v, cc=cc, t3a=t3a:
                                 v.tensor_tensor(
                                     t3a[:, :], attn[:, cc, :], mb3[:, :],
                                     op=ALU.subtract),
                                 wait=[(st_sem[cc % 2], st_od[cc % 2]), "ve"])
                            p.op("vector", lambda v, cc=cc, t3a=t3a:
                                 v.tensor_tensor(
                                     t3a[:, :], t3a[:, :], rb3[:, :],
                                     op=ALU.mult), wait=["ve"])
                            p.op("vector", lambda v, cc=cc, t3a=t3a:
                                 v.tensor_tensor(
                                     t3a[:, :], t3a[:, :], xqsub[:, cc, :],
                                     op=ALU.add), wait=["ve"])
                            p.op("sync", lambda s, cc=cc, ssl=ssl, t3a=t3a:
                                 s.dma_start(y1v[:, cc, ssl], t3a[:, :]),
                                 inc=st_sem[cc % 2], inc_n=16, wait=["ve"])
                            st_od[cc % 2] = p.cnt(st_sem[cc % 2])
                    p.flush()

        p.barrier(("xdma", "wdma", "odma", "la0", "la1", "lb0", "lb1",
                   "lh0", "lh1", "st0", "st1", "pe", "ve", "se",
                   "c2d0", "c2d1", "cdma"))
        p.flush()

        with contextlib.ExitStack() as ph4:
            c4 = ph4.enter_context
            res = c4(nc.sbuf_tensor("res", [128, 16, FT], BF16))
            xtc = c4(nc.sbuf_tensor("xtc", [128, 16, FT], BF16))
            gexp = c4(nc.sbuf_tensor("gexp", [128, NEXP, 8, FT], BF16))
            hsb = c4(nc.sbuf_tensor("hsb", [128, 8, FT], BF16))
            shs = hsb
            shp_ = c4(nc.sbuf_tensor("shp", [128, 16, FT], BF16))
            ssb = c4(nc.sbuf_tensor("ssb", [128, FT], BF16))
            warena = [c4(nc.sbuf_tensor(f"warena{i}", [128, 6144], BF16))
                      for i in range(2)]
            wupq = [warena[i][:, 0:4096]
                    .rearrange("p (a b c) -> p a b c", b=16, c=128)
                    for i in range(2)]
            wswq = [warena[i][:, 4096:6144]
                    .rearrange("p (a b c) -> p a b c", b=8, c=128)
                    for i in range(2)]
            wdnB = [warena[i][:, 0:6144]
                    .rearrange("p (a b c) -> p a b c", b=8, c=128)
                    for i in range(2)]
            wsuB = [warena[0][:, 0:2048].rearrange("p (a b) -> p a b", b=128),
                    warena[0][:, 2048:4096].rearrange("p (a b) -> p a b", b=128),
                    warena[1][:, 0:2048].rearrange("p (a b) -> p a b", b=128)]
            walias = c4(nc.sbuf_tensor("walias", [128, 8192], BF16))
            wsa_sb = walias[:, 0:8192].rearrange("p (a b c) -> p a b c",
                                                 b=8, c=128)
            wsdB = [c4(nc.sbuf_tensor(f"wsd{i}", [128, 16, 128], BF16))
                    for i in range(2)]
            wg_sb = c4(nc.sbuf_tensor("wg_sb", [128, 2, 16, 8], BF16))
            bg_sb = c4(nc.sbuf_tensor("bg_sb", [8, 2], F32))
            bdn_sb = c4(nc.sbuf_tensor("bdn_sb", [NEXP, D], BF16))
            bup_sb = c4(nc.sbuf_tensor("bup_sb", [128, NEXP * 8], F32))
            bsw_sb = c4(nc.sbuf_tensor("bsw_sb", [128, NEXP * 8], F32))
            bsu_sb = c4(nc.sbuf_tensor("bsu_sb", [128, 16], F32))
            bsa_sb = c4(nc.sbuf_tensor("bsa_sb", [128, 8], F32))
            bsd_sb = c4(nc.sbuf_tensor("bsd_sb", [128, 16], F32))

            gt = c4(nc.sbuf_tensor("gt", [128, 4, 8], F32))
            gw1 = c4(nc.sbuf_tensor("gw1", [128, 4, 8], F32))
            gw2 = c4(nc.sbuf_tensor("gw2", [128, 4, 8], F32))
            gred = c4(nc.sbuf_tensor("gred", [128, 4, 1], F32))
            wsb = c4(nc.sbuf_tensor("wsb", [8, FT], BF16))
            wbc = walias[:, 0:NEXP * FT].rearrange("p (a b) -> p a b", b=FT)
            y1t = c4(nc.sbuf_tensor("y1t", [128, 16, FT], F32))
            osub = c4(nc.sbuf_tensor("osub", [128, FT], F32))
            gate = osub[0:8, :]
            w0 = gexp[:].rearrange("p a b c -> p (a b c)")[0:1, 21504:24576]
            sq4 = ssb
            mb4 = c4(nc.sbuf_tensor("mb4", [128, FT], F32))
            rb4 = c4(nc.sbuf_tensor("rb4", [128, FT], F32))
            t4a = c4(nc.sbuf_tensor("t4a", [128, FT], F32))
            t4b = hsb[:].rearrange("p a b -> p (a b)")[:, 0:1024].bitcast(F32)

            ps4_a = c4(nc.psum_tensor("ps4_a", [128, 512], F32))
            ps4_b = c4(nc.psum_tensor("ps4_b", [128, 512], F32))
            ps4_o = c4(nc.psum_tensor("ps4_o", [128, 512], F32))
            ps4_s1 = c4(nc.psum_tensor("ps4_s1", [1, 512], F32))
            ps4_s2 = c4(nc.psum_tensor("ps4_s2", [1, 512], F32))
            ps4_m = c4(nc.psum_tensor("ps4_m", [128, 512], F32))
            ps4_r = c4(nc.psum_tensor("ps4_r", [128, 512], F32))
            ps4_g = c4(nc.psum_tensor("ps4_g", [128, 512], F32))
            p.set_warm(ps4_s1[0:1, 0:1], ones_bf[0:1, 0:1], ones_bf[0:1, 0:1])

            p.op("sync", lambda sy: sy.dma_start(
                wg_sb[:].rearrange("p a b c -> p a (b c)"),
                wgb.rearrange("a p b c -> p a (b c)")), inc="wdma", inc_n=16)
            p.op("sync", lambda sy: sy.dma_start(bg_sb[:], bgt),
                 inc="wdma", inc_n=16)
            p.op("sync", lambda sy: sy.dma_start(bdn_sb[:], bdnb),
                 inc="wdma", inc_n=16)
            p.op("sync", lambda sy: sy.dma_start(bup_sb[:], bupt),
                 inc="wdma", inc_n=16)
            p.op("sync", lambda sy: sy.dma_start(bsw_sb[:], bswt),
                 inc="wdma", inc_n=16)
            p.op("sync", lambda sy: sy.dma_start(bsu_sb[:], bsut),
                 inc="wdma", inc_n=16)
            p.op("sync", lambda sy: sy.dma_start(bsa_sb[:], bsat),
                 inc="wdma", inc_n=16)
            p.op("sync", lambda sy: sy.dma_start(bsd_sb[:], bsdt),
                 inc="wdma", inc_n=16)
            p.barrier(("wdma",), engines=("tensor", "scalar", "vector"))

            up_sem = ("la0", "la1")
            sw_sem = ("lb0", "lb1")

            y1_x = [0]
            p.op("sync", lambda sy: sy.dma_start(
                y1t[:], y1v[:, :, 0:FT]), inc="xdma", inc_n=16)
            y1_x[0] = p.cnt("xdma")
            for ft in range(NFT):
                fsl = slice(ft * FT, ft * FT + FT)
                p.barrier(("pe", "ve", "se", "odma", "st0", "st1"),
                          engines=("sync",))
                p.wait("tensor", ("xdma", y1_x[0]))
                p.wait("vector", ("xdma", y1_x[0]))
                p.wait("scalar", ("xdma", y1_x[0]))
                for cc in range(16):
                    p.op("tensor", lambda t, cc=cc: t.matmul(
                        ps4_s1[:, :], ones_f[:, :], y1t[:, cc, :],
                        start=(cc == 0), stop=(cc == 15)))
                for cc in range(16):
                    p.op("scalar", lambda s, cc=cc: s.activation(
                        sq4[:, :], y1t[:, cc, :], ACTF.Square), wait=["pe"])
                    p.op("tensor", lambda t, cc=cc: t.matmul(
                        ps4_s2[:, :], ones_bf[:, :], sq4[:, :],
                        start=(cc == 0), stop=(cc == 15)), wait=["se"])
                ln_helper(ps4_s1, ps4_s2, ps4_m, ps4_r, mb4, rb4, FT)
                for cc in range(16):
                    p.op("vector", lambda v, cc=cc: v.tensor_tensor(
                        t4a[:, :], y1t[:, cc, :], mb4[:, :], op=ALU.subtract),
                        wait=(["ve", ("st0",), ("st1",)] if cc == 0
                              else ["ve"]))
                    p.op("vector", lambda v, cc=cc: v.tensor_tensor(
                        res[:, cc, :], t4a[:, :], rb4[:, :], op=ALU.mult),
                        wait=["ve"])
                    if cc % 4 == 3:
                        p.keepwarm(("ve",))

                for j in range(2):
                    xcur = res if j == 0 else xtc
                    for kc in range(16):
                        p.op("tensor", lambda t, kc=kc, j=j, xcur=xcur: t.matmul(
                            ps4_g[0:8, :], wg_sb[:, j, kc, :], xcur[:, kc, :],
                            start=(kc == 0), stop=(kc == 15)),
                            wait=(["ve", "se"] if kc == 0 else None))
                    p.op("scalar", lambda s, j=j: s.activation(
                        gate[:, :], ps4_g[0:8, :], ACTF.Identity,
                        bias=bg_sb[:, j:j + 1]), wait=["pe"])
                    for tt in range(4):
                        p.op("tensor", lambda t, tt=tt: t.transpose(
                            ps4_m[:, tt * 8:tt * 8 + 8],
                            gate[:, tt * 128:tt * 128 + 128],
                            ident32[0:8, 0:8]),
                            wait=(["se", "ve"] if tt == 0 else None))
                    p.op("vector", lambda v: v.tensor_copy(
                        gt[:].rearrange("p a b -> p (a b)"), ps4_m[:, 0:32]),
                        wait=["pe"])
                    p.op("vector", lambda v: v.tensor_reduce(
                        gred[:, :, :], gt[:, :, :], axis=AX.X, op=ALU.min),
                        wait=["ve"])
                    p.op("vector", lambda v: v.tensor_tensor(
                        gw1[:, :, :], gt[:, :, :],
                        gred[:, :, :].broadcast_to([128, 4, 8]),
                        op=ALU.is_equal), wait=["ve"])
                    p.op("vector", lambda v: v.scalar_tensor_tensor(
                        gw2[:, :, :], gw1[:, :, :], BIG, gt[:, :, :],
                        op0=ALU.mult, op1=ALU.add), wait=["ve"])
                    p.keepwarm(("ve",))
                    p.op("vector", lambda v: v.tensor_reduce(
                        gred[:, :, :], gw2[:, :, :], axis=AX.X, op=ALU.min),
                        wait=["ve"])
                    p.op("vector", lambda v: v.tensor_tensor(
                        gw2[:, :, :], gw2[:, :, :],
                        gred[:, :, :].broadcast_to([128, 4, 8]),
                        op=ALU.is_equal), wait=["ve"])
                    p.op("vector", lambda v: v.tensor_tensor(
                        gw1[:, :, :], gw1[:, :, :], gw2[:, :, :], op=ALU.add),
                        wait=["ve"])
                    p.op("vector", lambda v: v.tensor_scalar(
                        gw1[:, :, :], gw1[:, :, :], -1.0, 1.0,
                        op0=ALU.mult, op1=ALU.add), wait=["ve"])
                    p.op("vector", lambda v: v.tensor_reduce(
                        gred[:, :, :], gt[:, :, :], axis=AX.X, op=ALU.max),
                        wait=["ve"])
                    p.op("vector", lambda v: v.tensor_tensor(
                        gw2[:, :, :], gt[:, :, :],
                        gred[:, :, :].broadcast_to([128, 4, 8]),
                        op=ALU.subtract), wait=["ve"])
                    p.op("scalar", lambda s: s.activation(
                        gw2[:].rearrange("p a b -> p (a b)"),
                        gw2[:].rearrange("p a b -> p (a b)"), ACTF.Exp),
                        wait=["ve"])
                    p.keepwarm(("se",))
                    p.op("vector", lambda v: v.tensor_tensor(
                        gw2[:, :, :], gw2[:, :, :], gw1[:, :, :], op=ALU.mult),
                        wait=["se"])
                    p.op("vector", lambda v: v.tensor_reduce(
                        gred[:, :, :], gw2[:, :, :], axis=AX.X, op=ALU.add),
                        wait=["ve"])
                    p.op("vector", lambda v: v.reciprocal(
                        gred[:].rearrange("p a b -> p (a b)"),
                        gred[:].rearrange("p a b -> p (a b)")), wait=["ve"])
                    p.op("vector", lambda v: v.tensor_tensor(
                        gw2[:, :, :], gw2[:, :, :],
                        gred[:, :, :].broadcast_to([128, 4, 8]), op=ALU.mult),
                        wait=["ve"])
                    for tt in range(4):
                        p.op("tensor", lambda t, tt=tt: t.transpose(
                            ps4_g[0:8, tt * 128:tt * 128 + 128],
                            gw2[:].rearrange("p a b -> p (a b)")
                            [:, tt * 8:tt * 8 + 8], ident32[:, :]),
                            wait=(["ve", "se"] if tt == 0 else None))
                    p.op("scalar", lambda s: s.copy(wsb[:, :], ps4_g[0:8, :]),
                         wait=["pe"])
                    for n in range(NEXP):
                        p.op("sync", lambda sy, n=n: sy.dma_start(
                            w0[0:1, n * FT:(n + 1) * FT], wsb[n:n + 1, :]),
                            inc="lh0", inc_n=16,
                            wait=(["se", ("pe",)] if n == 0 else None))
                    p.wait("tensor", ("lh0", p.cnt("lh0")))
                    for n in range(NEXP):
                        p.op("tensor", lambda t, n=n: t.matmul(
                            ps4_m[:, :], ones1_bf[:, :],
                            w0[0:1, n * FT:(n + 1) * FT],
                            start=True, stop=True),
                            wait=(["ve"] if n > 0 else None))
                        p.op("vector", lambda v, n=n: v.tensor_copy(
                            wbc[:, n, :], ps4_m[:, :]), wait=["pe"])

                    up_cnt = [p.cnt("la0"), p.cnt("la1")]
                    sw_cnt = [p.cnt("lb0"), p.cnt("lb1")]

                    def load_upq(n, q, buf):
                        p.op("sync", lambda sy, n=n, q=q, buf=buf: sy.dma_start(
                            wupq[buf].rearrange("p a b c -> p a (b c)"),
                            wupb[n][:, q * 2:q * 2 + 2, :, :]
                            .rearrange("p a b c -> p a (b c)")),
                            inc=up_sem[buf], inc_n=16, wait=[("pe",)])
                        up_cnt[buf] = p.cnt(up_sem[buf])

                    def load_swq(n, q, buf):
                        p.op("sync", lambda sy, n=n, q=q, buf=buf: sy.dma_start(
                            wswq[buf].rearrange("p a b c -> p a (b c)"),
                            wswb[n][:, q * 2:q * 2 + 2, :, :]
                            .rearrange("p a b c -> p a (b c)")),
                            inc=sw_sem[buf], inc_n=16, wait=[("pe",)])
                        sw_cnt[buf] = p.cnt(sw_sem[buf])

                    load_upq(0 if j == 0 else 0, 0, 0) if False else None
                    load_upq(0, 0, 0)
                    load_swq(0, 0, 0)
                    bank_se = [p.cnt("se"), p.cnt("se")]
                    for n in range(NEXP):
                        for ec in range(8):
                            q, buf = ec // 2, (ec // 2) % 2
                            if ec % 2 == 0:
                                if ec < 6:
                                    load_upq(n, q + 1, buf ^ 1)
                                elif n + 1 < NEXP:
                                    load_upq(n + 1, 0, buf ^ 1)
                                p.wait("tensor", (up_sem[buf], up_cnt[buf]))
                            psx = ps4_a if ec % 2 == 0 else ps4_b
                            for kc in range(16):
                                p.op("tensor", lambda t, ec=ec, kc=kc, psx=psx,
                                     buf=buf, xcur=xcur: t.matmul(
                                         psx[:, :], wupq[buf][:, ec % 2, kc, :],
                                         xcur[:, kc, :],
                                         start=(kc == 0), stop=(kc == 15)),
                                     wait=([("se", bank_se[ec % 2])]
                                           if kc == 0 else None))
                            p.op("scalar", lambda s, ec=ec, n=n, psx=psx:
                                 s.activation(
                                     hsb[:, ec, :], psx[:, :], ACTF.Identity,
                                     bias=bup_sb[:, n * 8 + ec:n * 8 + ec + 1]),
                                 wait=["pe", ("ve",)])
                            bank_se[ec % 2] = p.cnt("se")
                        bank_se = [p.cnt("se"), p.cnt("se")]
                        for f2 in range(8):
                            q, buf = f2 // 2, (f2 // 2) % 2
                            if f2 % 2 == 0:
                                if f2 < 6:
                                    load_swq(n, q + 1, buf ^ 1)
                                elif n + 1 < NEXP:
                                    load_swq(n + 1, 0, buf ^ 1)
                                p.wait("tensor", (sw_sem[buf], sw_cnt[buf]))
                            psx = ps4_a if f2 % 2 == 0 else ps4_b
                            for ec in range(8):
                                p.op("tensor", lambda t, f2=f2, ec=ec, psx=psx,
                                     buf=buf: t.matmul(
                                         psx[:, :], wswq[buf][:, f2 % 2, ec, :],
                                         hsb[:, ec, :],
                                         start=(ec == 0), stop=(ec == 7)),
                                     wait=([("se", bank_se[f2 % 2])]
                                           if ec == 0 else None))
                            p.op("scalar", lambda s, f2=f2, n=n, psx=psx:
                                 s.activation(
                                     ssb[:, :], psx[:, :], ACTF.Silu,
                                     bias=bsw_sb[:, n * 8 + f2:n * 8 + f2 + 1]),
                                 wait=["pe", ("ve",)])
                            bank_se[f2 % 2] = p.cnt("se")
                            p.op("vector", lambda v, f2=f2: v.tensor_tensor(
                                ssb[:, :], ssb[:, :], hsb[:, f2, :],
                                op=ALU.mult), wait=["se"])
                            p.op("vector", lambda v, f2=f2, n=n: v.tensor_tensor(
                                gexp[:, n, f2, :], ssb[:, :], wbc[:, n, :],
                                op=ALU.mult), wait=["ve"])

                    su_cnt = [p.cnt("la0"), p.cnt("la1"), p.cnt("lh1")]
                    su_sem = ("la0", "la1", "lh1")
                    su_used = [p.cnt("pe")] * 3

                    def load_su(sc, buf):
                        p.op("sync", lambda sy, sc=sc, buf=buf: sy.dma_start(
                            wsuB[buf], wsub[:, sc, :, :]),
                            inc=su_sem[buf], inc_n=16,
                            wait=[("pe", su_used[buf])])
                        su_cnt[buf] = p.cnt(su_sem[buf])

                    load_su(0, 0)
                    load_su(1, 1)
                    p.op("sync", lambda sy: sy.dma_start(
                        wsa_sb.rearrange("p a b c -> p (a b c)"),
                        wsab.rearrange("p a b c -> p (a b c)")),
                        inc="lb0", inc_n=16, wait=[("ve",)])
                    wsa_ld = p.cnt("lb0")
                    bank_se = [p.cnt("se"), p.cnt("se")]
                    for slot in range(2):
                        for sc8 in range(8):
                            sc = slot * 8 + sc8
                            buf = sc % 3
                            psx = ps4_a if sc8 % 2 == 0 else ps4_b
                            if sc + 2 < 16:
                                load_su(sc + 2, (sc + 2) % 3)
                            p.wait("tensor", (su_sem[buf], su_cnt[buf]))
                            for kc in range(16):
                                p.op("tensor", lambda t, sc=sc, kc=kc, buf=buf,
                                     psx=psx, xcur=xcur: t.matmul(
                                         psx[:, :], wsuB[buf][:, kc, :],
                                         xcur[:, kc, :],
                                         start=(kc == 0), stop=(kc == 15)),
                                     wait=([("se", bank_se[sc8 % 2]), ("ve",)]
                                           if kc == 0 else None))
                            su_used[buf] = p.cnt("pe")
                            p.op("scalar", lambda s, sc=sc, sc8=sc8, psx=psx:
                                 s.activation(
                                     shs[:, sc8, :], psx[:, :], ACTF.Identity,
                                     bias=bsu_sb[:, sc:sc + 1]),
                                 wait=["pe", ("ve",)])
                            bank_se[sc8 % 2] = p.cnt("se")
                        bank_se = [p.cnt("se"), p.cnt("se")]
                        for ec in range(8):
                            psx = ps4_a if ec % 2 == 0 else ps4_b
                            for e2 in range(8):
                                wv = None
                                if e2 == 0:
                                    wv = [("se", bank_se[ec % 2]), ("ve",)]
                                    if ec == 0 and slot == 0:
                                        wv.append(("lb0", wsa_ld))
                                p.op("tensor", lambda t, ec=ec, e2=e2,
                                     psx=psx: t.matmul(
                                         psx[:, :], wsa_sb[:, ec, e2, :],
                                         shs[:, e2, :],
                                         start=(e2 == 0), stop=(e2 == 7)),
                                     wait=wv)
                            p.op("scalar", lambda s, ec=ec, psx=psx: s.activation(
                                ssb[:, :], psx[:, :], ACTF.Silu,
                                bias=bsa_sb[:, ec:ec + 1]), wait=["pe", ("ve",)])
                            bank_se[ec % 2] = p.cnt("se")
                            p.op("vector", lambda v, slot=slot, ec=ec:
                                 v.tensor_tensor(
                                     shp_[:, slot * 8 + ec, :], ssb[:, :],
                                     shs[:, ec, :], op=ALU.mult),
                                 wait=["se"])

                    dn_cnt = [p.cnt("la0"), p.cnt("la1")]

                    def load_dn(fc, buf):
                        p.op("sync", lambda sy, fc=fc, buf=buf: sy.dma_start(
                            wdnB[buf].rearrange("p a b c -> p (a b c)"),
                            wdnb[fc].rearrange("p a b c -> p (a b c)")),
                            inc=up_sem[buf], inc_n=16, wait=[("pe",)])
                        p.op("sync", lambda sy, fc=fc, buf=buf: sy.dma_start(
                            wsdB[buf][:].rearrange("p a b -> p (a b)"),
                            wsdb[fc].rearrange("p a b -> p (a b)")),
                            inc=up_sem[buf], inc_n=16)
                        dn_cnt[buf] = p.cnt(up_sem[buf])

                    load_dn(0, 0)
                    bank_se = [p.cnt("se"), p.cnt("se")]
                    ve_ready = p.cnt("ve")
                    x2_od = [p.cnt("st0"), p.cnt("st1")]
                    for fc in range(16):
                        buf = fc % 2
                        pso = ps4_o if fc % 2 == 0 else ps4_g
                        if fc + 1 < 16:
                            load_dn(fc + 1, buf ^ 1)
                        p.wait("tensor", (up_sem[buf], dn_cnt[buf]))
                        p.op("tensor", lambda t, fc=fc, pso=pso: t.matmul(
                            pso[:, :], bdn_sb[:, fc * 128:fc * 128 + 128],
                            wsb[0:NEXP, :], start=True, stop=False),
                            wait=[("ve", ve_ready), ("se", bank_se[fc % 2])])
                        for n in range(NEXP):
                            for ec in range(8):
                                p.op("tensor", lambda t, n=n, ec=ec, buf=buf,
                                     pso=pso: t.matmul(
                                         pso[:, :], wdnB[buf][:, n, ec, :],
                                         gexp[:, n, ec, :],
                                         start=False, stop=False))
                        for sc in range(16):
                            p.op("tensor", lambda t, sc=sc, buf=buf, pso=pso:
                                 t.matmul(
                                     pso[:, :], wsdB[buf][:, sc, :],
                                     shp_[:, sc, :],
                                     start=False, stop=(sc == 15)))
                        p.op("scalar", lambda s, fc=fc, pso=pso: s.activation(
                            osub[:, :], pso[:, :], ACTF.Identity,
                            bias=bsd_sb[:, fc:fc + 1]),
                            wait=["pe", ("ve",), ("xdma",)])
                        bank_se[fc % 2] = p.cnt("se")
                        if j == 0:
                            p.op("vector", lambda v, fc=fc: v.tensor_tensor(
                                xtc[:, fc, :], osub[:, :], res[:, fc, :],
                                op=ALU.add), wait=["se", ("pe",)])
                        else:
                            tx = t4a if fc % 2 == 0 else t4b
                            p.op("vector", lambda v, fc=fc, tx=tx:
                                 v.tensor_tensor(
                                     tx[:, :], osub[:, :], res[:, fc, :],
                                     op=ALU.add),
                                 wait=["se", (("st0", "st1")[fc % 2],
                                              x2_od[fc % 2])])
                            p.op("sync", lambda s, fc=fc, tx=tx: s.dma_start(
                                x2v[:, fc, :], tx[:, :]),
                                inc=("st0", "st1")[fc % 2],
                                inc_n=16, wait=["ve"])
                            x2_od[fc % 2] = p.cnt(("st0", "st1")[fc % 2])
                            p.op("tensor", lambda t, fc=fc, tx=tx: t.matmul(
                                ps4_s1[:, :], ones_f[:, :], tx[:, :],
                                start=(fc == 0), stop=(fc == 15)), wait=["ve"])
                            p.op("scalar", lambda s, fc=fc, tx=tx: s.activation(
                                sq4[:, :], tx[:, :], ACTF.Square), wait=["pe"])
                            p.op("tensor", lambda t, fc=fc: t.matmul(
                                ps4_s2[:, :], ones_bf[:, :], sq4[:, :],
                                start=(fc == 0), stop=(fc == 15)), wait=["se"])

                ln_helper(ps4_s1, ps4_s2, ps4_m, ps4_r, mb4, rb4, FT)
                st_od = [p.cnt("st0"), p.cnt("st1")]
                x2done = [("st0", p.cnt("st0")), ("st1", p.cnt("st1"))]
                for cc in range(16):
                    tx = t4a if cc % 2 == 0 else t4b
                    p.op("sync", lambda sy, cc=cc: sy.dma_start(
                        osub[:, :], x2v[:, cc, :]), inc="xdma", inc_n=16,
                        wait=([("ve",)] + (x2done if cc == 0 else [])))
                    p.op("vector", lambda v, cc=cc, tx=tx: v.tensor_tensor(
                        tx[:, :], osub[:, :], mb4[:, :], op=ALU.subtract),
                        wait=["xdma", "ve",
                              (("st0", "st1")[cc % 2], st_od[cc % 2])])
                    p.op("vector", lambda v, cc=cc, tx=tx: v.tensor_tensor(
                        tx[:, :], tx[:, :], rb4[:, :], op=ALU.mult),
                        wait=["ve"])
                    p.op("vector", lambda v, cc=cc, tx=tx: v.tensor_tensor(
                        tx[:, :], tx[:, :], y1t[:, cc, :], op=ALU.add),
                        wait=["ve"])
                    p.op("sync", lambda s, cc=cc, fsl=fsl, tx=tx: s.dma_start(
                        outv[:, cc, fsl], tx[:, :]),
                        inc=("st0", "st1")[cc % 2], inc_n=16,
                        wait=["ve"])
                    st_od[cc % 2] = p.cnt(("st0", "st1")[cc % 2])
                    p.keepwarm(("ve",))
                if ft + 1 < NFT:
                    nfsl = slice((ft + 1) * FT, (ft + 2) * FT)
                    p.op("sync", lambda sy, nfsl=nfsl: sy.dma_start(
                        y1t[:], y1v[:, :, nfsl]), inc="xdma", inc_n=16,
                        wait=[("ve",), ("odma",)])
                    y1_x[0] = p.cnt("xdma")
            p.flush()

        p.barrier(("odma", "st0", "st1"), engines=("sync",))
        p.flush()

    return nc



def _perm_pi():
    perm = []
    for c2 in range(8):
        perm += list(range(128 * (2 * c2), 128 * (2 * c2) + 64))
        perm += list(range(128 * (2 * c2 + 1), 128 * (2 * c2 + 1) + 64))
    for c2 in range(8):
        perm += list(range(128 * (2 * c2) + 64, 128 * (2 * c2) + 128))
        perm += list(range(128 * (2 * c2 + 1) + 64, 128 * (2 * c2 + 1) + 128))
    return np.array(perm)


def _wproj_p(wT):
    return np.ascontiguousarray(
        wT.reshape(16, 128, 16, 128).transpose(1, 2, 0, 3))


def _rope_tables(positions):
    half = 64
    theta = 10000.0 ** (-np.arange(half, dtype=np.float64) / half)
    ang = np.tile(theta, 2)[:, None] * positions[None, :].astype(np.float64)
    out = []
    for a in (np.cos(ang), np.sin(ang)):
        hi = a.astype(np.float32)
        lo = (a - hi.astype(np.float64)).astype(np.float32)
        out += [hi, lo]
    return out


def stage_inputs(inputs):
    import ml_dtypes
    f32 = np.float32
    x = np.asarray(inputs["L0_attn_x"], dtype=f32)
    Wq = np.asarray(inputs["L0_attn_Wq"], dtype=f32)
    Wk = np.asarray(inputs["L0_attn_Wk"], dtype=f32)
    Wv = np.asarray(inputs["L0_attn_Wv"], dtype=f32)
    Wo = np.asarray(inputs["L0_attn_Wo"], dtype=f32)
    Wg = np.asarray(inputs["L0_ffn_Wg"], dtype=f32)
    bg = np.asarray(inputs["L0_ffn_bg"], dtype=f32)
    Wup = np.asarray(inputs["L0_ffn_Wup"], dtype=f32)
    bup = np.asarray(inputs["L0_ffn_bup"], dtype=f32)
    Wsw = np.asarray(inputs["L0_ffn_Wsw"], dtype=f32)
    bsw = np.asarray(inputs["L0_ffn_bsw"], dtype=f32)
    Wdn = np.asarray(inputs["L0_ffn_Wdn"], dtype=f32)
    bdn = np.asarray(inputs["L0_ffn_bdn"], dtype=f32)
    Wsu = np.asarray(inputs["L0_ffn_Wsu"], dtype=f32)
    bsu = np.asarray(inputs["L0_ffn_bsu"], dtype=f32)
    Wsa = np.asarray(inputs["L0_ffn_Wsa"], dtype=f32)
    bsa = np.asarray(inputs["L0_ffn_bsa"], dtype=f32)
    Wsd = np.asarray(inputs["L0_ffn_Wsd"], dtype=f32)
    bsd = np.asarray(inputs["L0_ffn_bsd"], dtype=f32)

    perm = _perm_pi()
    shared = {
        "wqTp": _wproj_p(np.ascontiguousarray(Wq.T)[:, perm]),
        "wkTp": _wproj_p(np.ascontiguousarray(Wk.T)[:, perm]),
        "wvTp": np.ascontiguousarray(
            np.ascontiguousarray(Wv.T).reshape(16, 128, 4, 512)
            .transpose(1, 2, 0, 3)),
        "woTp": _wproj_p(np.ascontiguousarray(Wo.T)),
        "ident": np.eye(128, dtype=ml_dtypes.bfloat16),
        "ident32": np.eye(128, dtype=f32),
        "wgp": np.ascontiguousarray(Wg.reshape(2, 16, 128, 8).transpose(0, 2, 1, 3)),
        "bgt": np.ascontiguousarray(bg.T),
        "wupp": np.ascontiguousarray(
            Wup.reshape(NEXP, 16, 128, 8, 128).transpose(0, 2, 3, 1, 4)),
        "bupt": np.ascontiguousarray(
            bup.reshape(NEXP, 8, 128).transpose(2, 0, 1).reshape(128, NEXP * 8)),
        "wswp": np.ascontiguousarray(
            Wsw.reshape(NEXP, 8, 128, 8, 128).transpose(0, 2, 3, 1, 4)),
        "bswt": np.ascontiguousarray(
            bsw.reshape(NEXP, 8, 128).transpose(2, 0, 1).reshape(128, NEXP * 8)),
        "wdnp": np.ascontiguousarray(
            Wdn.reshape(NEXP, 8, 128, 16, 128).transpose(3, 2, 0, 1, 4)),
        "bdn": bdn,
        "wsup": np.ascontiguousarray(
            Wsu.reshape(16, 128, 16, 128).transpose(1, 2, 0, 3)),
        "bsut": np.ascontiguousarray(bsu.reshape(16, 128).T),
        "wsap": np.ascontiguousarray(
            Wsa.reshape(8, 128, 8, 128).transpose(1, 2, 0, 3)),
        "bsat": np.ascontiguousarray(bsa.reshape(8, 128).T),
        "wsdp": np.ascontiguousarray(
            Wsd.reshape(16, 128, 16, 128).transpose(2, 1, 0, 3)),
        "bsdt": np.ascontiguousarray(bsd.reshape(16, 128).T),
    }
    chi, clo, shi, slo = _rope_tables(np.arange(NSEQ))
    shared.update(ckhi=chi, cklo=clo, skhi=shi, sklo=slo)

    in_maps = []
    pos_per_core = []
    xcT = [np.ascontiguousarray(x[b].T) for b in range(NB)]
    for c in range(8):
        b, r = c // 4, c % 4
        tiles = r + 4 * np.arange(16)
        pos = (tiles[:, None] * 128 + np.arange(128)[None, :]).reshape(-1)
        pos_per_core.append((b, pos))
        cq_hi, cq_lo, sq_hi, sq_lo = _rope_tables(pos)
        jj = np.arange(16)[None, :, None]
        kp = np.arange(128)[:, None, None]
        qq = np.arange(512)[None, None, :]
        reltile = r + 4 * (qq // 128)
        m = (jj < reltile) | ((jj == reltile) & (kp <= (qq % 128)))
        im = dict(shared)
        im.update(
            xc=xcT[b],
            xq=np.ascontiguousarray(xcT[b][:, pos]),
            cqhi=cq_hi, cqlo=cq_lo, sqhi=sq_hi, sqlo=sq_lo,
            dmask=m.astype(ml_dtypes.bfloat16),
        )
        in_maps.append(im)
    return in_maps, pos_per_core


def kernel(**inputs):
    in_maps, pos_per_core = stage_inputs(inputs)
    nc = build_nc()
    res = run_bass_kernel_spmd(nc, in_maps, core_ids=list(range(8)))
    full = np.empty((NB, NSEQ, D), dtype=np.float32)
    for c in range(8):
        b, pos = pos_per_core[c]
        full[b, pos, :] = np.asarray(res.results[c]["out"], dtype=np.float32).T
    return full
